# version 01
# baseline (speedup 1.0000x reference)
import numpy as np
import ml_dtypes
import concourse.bass as bass
import concourse.mybir as mybir
from concourse.bass_utils import run_bass_kernel_spmd

F32 = mybir.dt.float32
BF16 = mybir.dt.bfloat16
F16 = mybir.dt.float16
FP8 = mybir.dt.float8e4
U8 = mybir.dt.uint8
Act = mybir.ActivationFunctionType
Alu = mybir.AluOpType
AX = mybir.AxisListType
RG = [list(range(8))]

BF = ml_dtypes.bfloat16
F8 = ml_dtypes.float8_e4m3

P = 128
D = 2048
TOK = 1024
TB = TOK // P
DC = D // P
S = 2048
B = 4
NH = 16
HD = 128
I0 = 5632
I1 = 8192
I0P = 768
INV_SQRT_HD = float(1.0 / np.sqrt(np.float32(HD)))
NEG_BIG = float(np.float32(-1e30))

N_CTRL = 130156
_SPLINE_CACHE = {}

def spline_consts():
    if "v" not in _SPLINE_CACHE:
        n = I1 * D
        t = np.linspace(0.0, N_CTRL - 1.0, n, dtype=np.float64)
        i = np.clip(np.floor(t).astype(np.int64), 0, N_CTRL - 2)
        f = (t - i).astype(np.float32)
        idx = np.stack([np.clip(i + k, 0, N_CTRL - 1) for k in (-1, 0, 1, 2)], axis=0)
        f2, f3 = f * f, f * f * f
        bk = np.stack([
            0.5 * (-f + 2 * f2 - f3),
            0.5 * (2.0 - 5 * f2 + 3 * f3),
            0.5 * (f + 4 * f2 - 3 * f3),
            0.5 * (-f2 + f3),
        ]).astype(np.float32)
        _SPLINE_CACHE["v"] = (idx, bk)
    return _SPLINE_CACHE["v"]

def host_in_maps(inp):
    idx, bk = spline_consts()
    n = I1 * D
    R = n // 8
    invp = np.argsort(np.asarray(inp["L0_ffn_down_reorder_index"]))
    wd2 = np.ascontiguousarray(np.asarray(inp["L0_ffn_w_down"])[:, invp])
    wg = np.asarray(inp["L0_ffn_w_gate"])
    wu = np.asarray(inp["L0_ffn_w_up"])
    x = np.asarray(inp["L0_attn_x"])
    wq, wk, wv = (np.asarray(inp[f"L0_attn_w{t}"]) for t in "qkv")
    bq, bkk, bv = (np.asarray(inp[f"L0_attn_b{t}"]) for t in "qkv")
    wo = np.ascontiguousarray(np.asarray(inp["L0_attn_wo"]))
    bo = np.asarray(inp["L0_attn_bo"])
    cps = [np.asarray(inp["L1_ffn_gate_cp"]), np.asarray(inp["L1_ffn_up_cp"]),
           np.asarray(inp["L1_ffn_down_cp"])]

    ident = np.eye(P, dtype=np.float32)
    ones_row = np.ones((1, P), dtype=np.float32)
    masks = np.zeros((4, P, 512), dtype=np.float32)
    for off in range(4):
        qi = off * 128 + np.arange(P)[:, None]
        kj = np.arange(512)[None, :]
        masks[off] = np.where(qi >= kj, 0.0, NEG_BIG)
    masks = np.ascontiguousarray(masks.transpose(1, 0, 2))

    def dcn(a):
        return np.ascontiguousarray(np.asarray(a).reshape(DC, P, 8).transpose(1, 0, 2))
    l1_small = {
        "arl": dcn(inp["L1_attn_A_real_log"]), "aim": dcn(inp["L1_attn_A_imag"]),
        "cre": dcn(inp["L1_attn_C_re"]), "cim": dcn(inp["L1_attn_C_im"]),
        "ldt": np.ascontiguousarray(np.asarray(inp["L1_attn_log_dt"]).reshape(DC, P).T),
        "dpar": np.ascontiguousarray(np.asarray(inp["L1_attn_D_param"]).reshape(1, D)),
    }

    in_maps = []
    for c in range(8):
        m = {}
        m["x"] = np.ascontiguousarray(x[:, 256*c:256*(c+1), :])
        for nm, w, b2 in (("q", wq, bq), ("k", wk, bkk), ("v", wv, bv)):
            m[f"w{nm}"] = np.ascontiguousarray(w[256*c:256*(c+1), :])
            m[f"b{nm}"] = np.ascontiguousarray(b2[256*c:256*(c+1)].reshape(2, P))
        m["wo"] = wo
        m["bo"] = np.ascontiguousarray(bo.reshape(1, D))

        def padrows(w, r0, r1, tot):
            sl = w[r0:min(r1, tot), :]
            if sl.shape[0] < r1 - r0:
                sl = np.concatenate([sl, np.zeros((r1 - r0 - sl.shape[0], w.shape[1]), w.dtype)])
            return np.ascontiguousarray(sl)
        m["wg_sh"] = padrows(wg, I0P*c, I0P*(c+1), I0)
        m["wu_sh"] = padrows(wu, I0P*c, I0P*(c+1), I0)
        m["wd_sh"] = np.ascontiguousarray(wd2[256*c:256*(c+1), :])

        sl = slice(R*c, R*(c+1))
        for mi, cp in enumerate(cps):
            for k in range(4):
                m[f"tap{mi}_{k}"] = np.ascontiguousarray(cp[idx[k, sl]]).astype(np.float32)
        for k in range(4):
            m[f"bk{k}"] = np.ascontiguousarray(bk[k, sl])
        m.update(l1_small)
        m["ident"] = ident
        m["ones_row"] = ones_row
        m["masks"] = masks
        in_maps.append(m)
    return in_maps

class Prog:
    def __init__(self, dump=()):
        self.nc = bass.Bass("TRN2", target_bir_lowering=False, detect_race_conditions=False)
        self.dump = set(dump)
        self.dumps = []
        self.sems = {}
        self.cnt = {}
        self.dma_names = set()

    def sem(self, name):
        if name not in self.sems:
            self.sems[name] = self.nc.alloc_semaphore(name)
            self.cnt[name] = 0
        return self.sems[name]

    def inc(self, instr, name, k=1):
        instr.then_inc(self.sem(name), k)
        self.cnt[name] += k
        if k >= 16 or name.startswith("ag"):
            self.dma_names.add(name)
        return self.cnt[name]

    def val(self, name):
        self.sem(name)
        return self.cnt[name]

    def fence(self, eng, op, name="hz"):
        n = self.inc(op, name)
        eng.wait_ge(self.sem(name), n)
        return op

    def dram(self, name, shape, dt):
        return self.nc.dram_tensor(name, list(shape), dt, kind="Internal").ap()

    def inp(self, name, shape, dt):
        return self.nc.dram_tensor(name, list(shape), dt, kind="ExternalInput").ap()

    def out(self, name, shape, dt):
        return self.nc.dram_tensor(name, list(shape), dt, kind="ExternalOutput").ap()

    def maybe_dump(self, name, ap):
        if name in self.dump:
            self.dumps.append((name, ap))

    def emit_dumps(self):
        if not self.dumps:
            return
        nc = self.nc
        with nc.Block() as blk:
            @blk.sync
            def _(sync):
                for name, ap in self.dumps:
                    o = self.out("dump_" + name, ap.shape, ap.dtype)
                    idxs = tuple(slice(None) for _ in ap.shape)
                    self.inc(sync.dma_start(out=o[idxs], in_=ap[idxs]), "dumpst", 16)
                sync.wait_ge(self.sem("dumpst"), self.val("dumpst"))


def barrier(pr):
    pr.nc.all_engine_barrier()


def recycle(pr, keep=()):
    nc = pr.nc
    names = [n for n in list(pr.sems) if n not in keep and n not in pr.dma_names]
    if not names:
        return
    with nc.Block() as blk:
        @blk.gpsimd
        def _(g):
            for n in names:
                g.sem_clear(pr.sems[n])
    pr.nc.all_engine_barrier()
    for n in names:
        nc.release_semaphore(pr.sems[n])
        del pr.sems[n]
        del pr.cnt[n]


def emit_ln_xnt(pr, d_x, d_xn, d_xnt_h, d_xnt_l, s_ident, tag="a"):
    nc = pr.nc
    with (
        nc.sbuf_tensor(f"{tag}_x", [P, 2, D], F32) as s_x,
        nc.sbuf_tensor(f"{tag}_st", [P, 6], F32) as s_st,
        nc.sbuf_tensor(f"{tag}_xn", [P, 2, D], F32) as s_xn,
        nc.sbuf_tensor(f"{tag}_th", [P, 2, D], F16) as s_th,
        nc.sbuf_tensor(f"{tag}_tl", [P, 2, D], BF16) as s_tl,
        nc.psum_tensor(f"{tag}_ps", [P, DC * 128], F32) as ps_t,
        nc.Block() as blk,
    ):
        @blk.sync
        def _(sync):
            for tb in range(TB):
                b, sh = divmod(tb, 2)
                if tb >= 2:
                    sync.wait_ge(pr.sem(f"{tag}_fin"), tb - 1)
                xin = (d_x[b, sh*P:(sh+1)*P, :] if len(d_x.shape) == 3
                       else d_x[tb*P:(tb+1)*P, :])
                pr.inc(sync.dma_start(out=s_x[:, tb % 2, :], in_=xin),
                       f"{tag}_ldx{tb % 2}", 16)

        @blk.vector
        def _(v):
            def evac(te):
                if te < 0:
                    return
                if te >= 2:
                    v.wait_ge(pr.sem(f"{tag}_fin"), te - 1)
                v.wait_ge(pr.sem(f"{tag}_hi"), te + 1)
                for dc in range(DC):
                    op = v.tensor_tensor(out=s_tl[:, te % 2, dc*128:(dc+1)*128],
                                         in0=ps_t[:, dc*128:(dc+1)*128],
                                         in1=s_th[:, te % 2, dc*128:(dc+1)*128],
                                         op=Alu.subtract)
                    if dc == DC - 1:
                        pr.inc(op, f"{tag}_ev")
            for tb in range(TB):
                buf = tb % 2
                v.wait_ge(pr.sem(f"{tag}_ldx{buf}"), 16 * (tb // 2 + 1))
                if tb >= 2:
                    v.wait_ge(pr.sem(f"{tag}_tr"), tb - 1)
                    v.wait_ge(pr.sem(f"{tag}_fin"), tb - 1)
                xt = s_x[:, buf, :]
                F = lambda op: pr.fence(v, op, f"{tag}_hz")
                F(v.tensor_reduce(out=s_st[:, 0:1], in_=xt, axis=AX.X, op=Alu.add))
                v.tensor_tensor(out=s_xn[:, buf, :], in0=xt, in1=xt, op=Alu.mult)
                F(v.tensor_reduce(out=s_st[:, 1:2], in_=s_xn[:, buf, :], axis=AX.X, op=Alu.add))
                F(v.tensor_scalar(out=s_st[:, 0:1], in0=s_st[:, 0:1], scalar1=1.0/D,
                                  scalar2=None, op0=Alu.mult))
                F(v.tensor_scalar(out=s_st[:, 1:2], in0=s_st[:, 1:2], scalar1=1.0/D,
                                  scalar2=None, op0=Alu.mult))
                F(v.tensor_tensor(out=s_st[:, 2:3], in0=s_st[:, 0:1],
                                  in1=s_st[:, 0:1], op=Alu.mult))
                F(v.tensor_tensor(out=s_st[:, 1:2], in0=s_st[:, 1:2], in1=s_st[:, 2:3],
                                  op=Alu.subtract))
                pr.inc(v.tensor_scalar(out=s_st[:, 1:2], in0=s_st[:, 1:2], scalar1=1e-5,
                                       scalar2=None, op0=Alu.add), f"{tag}_var")
                v.wait_ge(pr.sem(f"{tag}_sqrt"), tb + 1)
                F(v.reciprocal(out=s_st[:, 4:5], in_=s_st[:, 3:4]))
                F(v.tensor_tensor(out=s_st[:, 5:6], in0=s_st[:, 1:2], in1=s_st[:, 4:5],
                                  op=Alu.mult))
                F(v.tensor_tensor(out=s_st[:, 2:3], in0=s_st[:, 3:4], in1=s_st[:, 5:6],
                                  op=Alu.add))
                F(v.tensor_scalar(out=s_st[:, 2:3], in0=s_st[:, 2:3],
                                  scalar1=0.5, scalar2=None, op0=Alu.mult))
                F(v.reciprocal(out=s_st[:, 4:5], in_=s_st[:, 2:3]))
                pr.inc(v.tensor_scalar(out=s_xn[:, buf, :], in0=xt,
                                       scalar1=s_st[:, 0:1], scalar2=s_st[:, 4:5],
                                       op0=Alu.subtract, op1=Alu.mult), f"{tag}_xn")
                evac(tb - 1)
            evac(TB - 1)

        @blk.scalar
        def _(sc):
            def his(te):
                if te < 0:
                    return
                if te >= 2:
                    sc.wait_ge(pr.sem(f"{tag}_fin"), te - 1)
                sc.wait_ge(pr.sem(f"{tag}_tr"), te + 1)
                for dc in range(DC):
                    op = sc.activation(out=s_th[:, te % 2, dc*128:(dc+1)*128],
                                       in_=ps_t[:, dc*128:(dc+1)*128],
                                       func=Act.Copy)
                    if dc == DC - 1:
                        pr.inc(op, f"{tag}_hi")
            for tb in range(TB):
                sc.wait_ge(pr.sem(f"{tag}_var"), tb + 1)
                pr.inc(sc.activation(out=s_st[:, 3:4], in_=s_st[:, 1:2], func=Act.Sqrt),
                       f"{tag}_sqrt")
                his(tb - 1)
            his(TB - 1)

        @blk.tensor
        def _(t):
            for tb in range(TB):
                buf = tb % 2
                t.wait_ge(pr.sem(f"{tag}_xn"), tb + 1)
                if tb >= 1:
                    t.wait_ge(pr.sem(f"{tag}_ev"), tb)
                for dc in range(DC):
                    op = t.transpose(ps_t[:, dc*128:(dc+1)*128],
                                     s_xn[:, buf, dc*128:(dc+1)*128], s_ident[:, :])
                    if dc == DC - 1:
                        pr.inc(op, f"{tag}_tr")

        @blk.gpsimd
        def _(g):
            for tb in range(TB):
                buf = tb % 2
                g.wait_ge(pr.sem(f"{tag}_xn"), tb + 1)
                pr.inc(g.dma_start(out=d_xn[tb*128:(tb+1)*128, :], in_=s_xn[:, buf, :]),
                       f"{tag}_stxn", 16)
                g.wait_ge(pr.sem(f"{tag}_ev"), tb + 1)
                pr.inc(g.dma_start(
                    out=d_xnt_h.rearrange("(c p) t -> p c t", p=P)[:, :, tb*128:(tb+1)*128],
                    in_=s_th[:, buf, :].rearrange("p (c q) -> p c q", q=128)), f"{tag}_stt", 16)
                pr.inc(g.dma_start(
                    out=d_xnt_l.rearrange("(c p) t -> p c t", p=P)[:, :, tb*128:(tb+1)*128],
                    in_=s_tl[:, buf, :].rearrange("p (c q) -> p c q", q=128)), f"{tag}_stt", 16)
                g.wait_ge(pr.sem(f"{tag}_stt"), pr.val(f"{tag}_stt"))
                g.wait_ge(pr.sem(f"{tag}_stxn"), pr.val(f"{tag}_stxn"))
                g.sem_inc(pr.sem(f"{tag}_fin"), 1)
                pr.cnt[f"{tag}_fin"] += 1



def emit_wsplit(pr, srcs, tag="b1"):
    nc = pr.nc
    C = srcs[0][0].shape[1]
    with (
        nc.sbuf_tensor(f"{tag}_w", [P, 2, C], F32) as s_w,
        nc.sbuf_tensor(f"{tag}_h", [P, 2, C], F16) as s_h,
        nc.sbuf_tensor(f"{tag}_l", [P, 2, C], BF16) as s_l,
        nc.Block() as blk,
    ):
        tiles = []
        for (src, dh, dl) in srcs:
            assert src.shape[1] == C
            for rb in range(src.shape[0] // P):
                tiles.append((src, dh, dl, rb))

        @blk.sync
        def _(sync):
            for i, (src, dh, dl, rb) in enumerate(tiles):
                if i >= 2:
                    sync.wait_ge(pr.sem(f"{tag}_fin"), i - 1)
                pr.inc(sync.dma_start(out=s_w[:, i % 2, :],
                                      in_=src[rb*P:(rb+1)*P, :]), f"{tag}_ld{i % 2}", 16)

        @blk.vector
        def _(v):
            for i, (src, dh, dl, rb) in enumerate(tiles):
                buf = i % 2
                v.wait_ge(pr.sem(f"{tag}_ld{buf}"), 16 * (i // 2 + 1))
                v.tensor_copy(out=s_h[:, buf, :], in_=s_w[:, buf, :])
                pr.inc(v.tensor_tensor(out=s_l[:, buf, :], in0=s_w[:, buf, :],
                                       in1=s_h[:, buf, :], op=Alu.subtract), f"{tag}_sp")

        @blk.gpsimd
        def _(g):
            for i, (src, dh, dl, rb) in enumerate(tiles):
                buf = i % 2
                g.wait_ge(pr.sem(f"{tag}_sp"), i + 1)
                pr.inc(g.dma_start(out=dh[rb*P:(rb+1)*P, :], in_=s_h[:, buf, :]), f"{tag}_st", 16)
                pr.inc(g.dma_start(out=dl[rb*P:(rb+1)*P, :], in_=s_l[:, buf, :]), f"{tag}_st", 16)
                g.wait_ge(pr.sem(f"{tag}_st"), pr.val(f"{tag}_st"))
                g.sem_inc(pr.sem(f"{tag}_fin"), 1)
                pr.cnt[f"{tag}_fin"] += 1


def emit_qkv(pr, g_xnt_h, g_xnt_l, wspl, bias_in, dq, dk, dvn, s_ident):
    nc = pr.nc
    TKC = 16
    PROJ = ("q", "k", "v")
    with (
        nc.sbuf_tensor("b2_wTh", [P, 3, 2, DC, P], F16) as s_wTh,
        nc.sbuf_tensor("b2_wTl", [P, 3, 2, DC, P], BF16) as s_wTl,
        nc.sbuf_tensor("b2_xh", [P, 2, DC, 512], F16) as s_xh,
        nc.sbuf_tensor("b2_xl", [P, 2, DC, 512], BF16) as s_xl,
        nc.sbuf_tensor("b2_b", [P, 3, 2], F32) as s_b,
        nc.sbuf_tensor("b2_qh", [P, 2, 512], F16) as s_qh,
        nc.sbuf_tensor("b2_ql", [P, 2, 512], BF16) as s_ql,
        nc.sbuf_tensor("b2_v32", [P, 2, 512], F32) as s_v32,
        nc.sbuf_tensor("b2_vh", [P, 2, 512], F16) as s_vh,
        nc.sbuf_tensor("b2_vl", [P, 2, 512], BF16) as s_vl,
        nc.psum_tensor("b2_ps", [P, 2, 512], F32) as ps,
        nc.psum_tensor("b2_pt", [P, 2, 512], F32) as ps_vt,
        nc.Block() as blk,
    ):
        def qk_m(tk, pi, ob):
            return tk * 4 + pi * 2 + ob

        @blk.scalar
        def _(sc):
            for pi, pn in enumerate(PROJ):
                dh, dl = wspl[pn]
                for ob in range(2):
                    for dc in range(DC):
                        pr.inc(sc.dma_start_transpose(
                            out=s_wTh[:, pi, ob, dc, :],
                            in_=dh[ob*P:(ob+1)*P, dc*P:(dc+1)*P]), "b2_wld", 16)
                        pr.inc(sc.dma_start_transpose(
                            out=s_wTl[:, pi, ob, dc, :],
                            in_=dl[ob*P:(ob+1)*P, dc*P:(dc+1)*P]), "b2_wld", 16)
            for pi, pn in enumerate(PROJ):
                for ob in range(2):
                    pr.inc(sc.dma_start(out=s_b[:, pi, ob:ob+1],
                                        in_=bias_in[pn][ob:ob+1, :].transpose([1, 0])),
                           "b2_wld", 16)
            sc.wait_ge(pr.sem("b2_wld"), pr.val("b2_wld"))
            sc.sem_inc(pr.sem("b2_wrdy"), 1)
            pr.cnt["b2_wrdy"] += 1
            for tk in range(TKC):
                for ob in range(2):
                    gv = tk * 2 + ob
                    sc.wait_ge(pr.sem("b2_vtr"), gv + 1)
                    if gv >= 2:
                        sc.wait_ge(pr.sem("b2_gv"), gv - 1)
                    pr.inc(sc.activation(out=s_vh[:, gv % 2, :], in_=ps_vt[:, gv % 2, :],
                                         func=Act.Copy), "b2_vhi")

        @blk.sync
        def _(sync):
            for tk in range(TKC):
                r, half = divmod(tk, 2)
                if tk >= 2:
                    sync.wait_ge(pr.sem("b2_mm"), (tk - 1) * 6)
                pr.inc(sync.dma_start(
                    out=s_xh[:, tk % 2, :, :],
                    in_=g_xnt_h[r, :, half*512:(half+1)*512].rearrange("(c p) t -> p c t", p=P)),
                    f"b2_x{tk % 2}", 16)
                pr.inc(sync.dma_start(
                    out=s_xl[:, tk % 2, :, :],
                    in_=g_xnt_l[r, :, half*512:(half+1)*512].rearrange("(c p) t -> p c t", p=P)),
                    f"b2_x{tk % 2}", 16)

        @blk.tensor
        def _(t):
            t.wait_ge(pr.sem("b2_wld"), pr.val("b2_wld"))
            for tk in range(TKC):
                t.wait_ge(pr.sem(f"b2_x{tk % 2}"), 32 * (tk // 2 + 1))
                for pi, pn in enumerate(PROJ):
                    for ob in range(2):
                        g = tk * 6 + pi * 2 + ob
                        if g >= 2:
                            gp = g - 2
                            tkp, jp = divmod(gp, 6)
                            if jp < 4:
                                t.wait_ge(pr.sem("b2_elo"), tkp * 4 + jp + 1)
                            else:
                                t.wait_ge(pr.sem("b2_ef32"), tkp * 2 + (jp - 4) + 1)
                        mm = None
                        for dc in range(DC):
                            first, last = dc == 0, dc == DC - 1
                            wh = s_wTh[:, pi, ob, dc, :]
                            wl = s_wTl[:, pi, ob, dc, :]
                            xh = s_xh[:, tk % 2, dc, :]
                            xl = s_xl[:, tk % 2, dc, :]
                            t.matmul(ps[:, g % 2, :], wh, xh, start=first, stop=False)
                            t.matmul(ps[:, g % 2, :], wh, xl, start=False, stop=False)
                            mm = t.matmul(ps[:, g % 2, :], wl, xh, start=False, stop=last)
                        pr.inc(mm, "b2_mm")
                for ob in range(2):
                    gv = tk * 2 + ob
                    t.wait_ge(pr.sem("b2_ef32"), gv + 1)
                    if gv >= 2:
                        t.wait_ge(pr.sem("b2_vlo"), gv - 1)
                    mm = None
                    for q4 in range(4):
                        mm = t.transpose(ps_vt[:, gv % 2, q4*P:(q4+1)*P],
                                         s_v32[:, gv % 2, q4*P:(q4+1)*P], s_ident[:, :])
                    pr.inc(mm, "b2_vtr")

        @blk.vector
        def _(v):
            v.wait_ge(pr.sem("b2_wrdy"), 1)
            for tk in range(TKC):
                for pi, pn in enumerate(PROJ):
                    for ob in range(2):
                        g = tk * 6 + pi * 2 + ob
                        pb = g % 2
                        v.wait_ge(pr.sem("b2_mm"), g + 1)
                        if pn == "v":
                            gv = tk * 2 + ob
                            if gv >= 2:
                                v.wait_ge(pr.sem("b2_vtr"), gv - 1)
                            op = v.tensor_scalar(out=s_v32[:, gv % 2, :], in0=ps[:, pb, :],
                                                 scalar1=s_b[:, pi, ob:ob+1], scalar2=None,
                                                 op0=Alu.add)
                            pr.inc(op, "b2_ef32")
                            continue
                        m = qk_m(tk, pi, ob)
                        if m >= 2:
                            v.wait_ge(pr.sem("b2_gq"), m - 1)
                        v.tensor_scalar(out=s_qh[:, m % 2, :], in0=ps[:, pb, :],
                                        scalar1=s_b[:, pi, ob:ob+1], scalar2=None,
                                        op0=Alu.add)
                        op = v.scalar_tensor_tensor(out=s_ql[:, m % 2, :], in0=ps[:, pb, :],
                                                    scalar=s_b[:, pi, ob:ob+1],
                                                    in1=s_qh[:, m % 2, :],
                                                    op0=Alu.add, op1=Alu.subtract)
                        pr.inc(op, "b2_elo")
                for ob in range(2):
                    gv = tk * 2 + ob
                    v.wait_ge(pr.sem("b2_vhi"), gv + 1)
                    pr.inc(v.tensor_tensor(out=s_vl[:, gv % 2, :],
                                           in0=ps_vt[:, gv % 2, :],
                                           in1=s_vh[:, gv % 2, :],
                                           op=Alu.subtract), "b2_vlo")

        @blk.gpsimd
        def _(g_):
            for tk in range(TKC):
                for pi, pn in enumerate(PROJ[:2]):
                    for ob in range(2):
                        m = qk_m(tk, pi, ob)
                        dsth, dstl = (dq if pn == "q" else dk)
                        g_.wait_ge(pr.sem("b2_elo"), m + 1)
                        pr.inc(g_.dma_start(out=dsth[ob*P:(ob+1)*P, tk*512:(tk+1)*512],
                                            in_=s_qh[:, m % 2, :]), "b2_stq", 16)
                        pr.inc(g_.dma_start(out=dstl[ob*P:(ob+1)*P, tk*512:(tk+1)*512],
                                            in_=s_ql[:, m % 2, :]), "b2_stq", 16)
                        g_.wait_ge(pr.sem("b2_stq"), pr.val("b2_stq"))
                        g_.sem_inc(pr.sem("b2_gq"), 1)
                        pr.cnt["b2_gq"] += 1
                for ob in range(2):
                    gv = tk * 2 + ob
                    g_.wait_ge(pr.sem("b2_vlo"), gv + 1)
                    pr.inc(g_.dma_start(
                        out=dvn[0].rearrange("(tk q4 p) o -> p tk q4 o", p=P, q4=4)[:, tk, :, ob*P:(ob+1)*P],
                        in_=s_vh[:, gv % 2, :].rearrange("p (q4 o) -> p q4 o", o=P)), "b2_stv", 16)
                    pr.inc(g_.dma_start(
                        out=dvn[1].rearrange("(tk q4 p) o -> p tk q4 o", p=P, q4=4)[:, tk, :, ob*P:(ob+1)*P],
                        in_=s_vl[:, gv % 2, :].rearrange("p (q4 o) -> p q4 o", o=P)), "b2_stv", 16)
                    g_.wait_ge(pr.sem("b2_stv"), pr.val("b2_stv"))
                    g_.sem_inc(pr.sem("b2_gv"), 1)
                    pr.cnt["b2_gv"] += 1



def emit_attn(pr, dq, dk, dvn, d_at, s_ident, i_masks):
    nc = pr.nc
    C = INV_SQRT_HD
    with (
        nc.sbuf_tensor("b3_qh", [P, 2, 8, 256], F16) as s_Qh,
        nc.sbuf_tensor("b3_ql", [P, 2, 8, 256], BF16) as s_Ql,
        nc.sbuf_tensor("b3_kh", [P, 2, 8, 256], F16) as s_Kh,
        nc.sbuf_tensor("b3_kl", [P, 2, 8, 256], BF16) as s_Kl,
        nc.sbuf_tensor("b3_vh", [P, 2, 8, 2, P], F16) as s_Vh,
        nc.sbuf_tensor("b3_vl", [P, 2, 8, 2, P], BF16) as s_Vl,
        nc.sbuf_tensor("b3_mask", [P, 4, 512], F32) as s_mask,
        nc.sbuf_tensor("b3_p", [P, 2048], F32) as s_P,
        nc.sbuf_tensor("b3_pn", [P, 2048], F32) as s_Pn,
        nc.sbuf_tensor("b3_pth", [P, 16, 512], F16) as s_PTh,
        nc.sbuf_tensor("b3_ptl", [P, 16, 512], BF16) as s_PTl,
        nc.sbuf_tensor("b3_st", [P, 4], F32) as s_st,
        nc.sbuf_tensor("b3_at", [P, 2, 512], F32) as s_at,
        nc.psum_tensor("b3_psc", [P, 2048], F32) as ps_sc,
        nc.psum_tensor("b3_ppt", [P, 2, 512], F32) as ps_pt,
        nc.psum_tensor("b3_ppv", [P, 512], F32) as ps_pv,
        nc.Block() as blk,
    ):
        NR = 8
        def rqt(r, qt):
            return r * 16 + qt

        @blk.sync
        def _(sync):
            pr.inc(sync.dma_start(out=s_mask[:, :, :], in_=i_masks[:, :, :]), "b3_mld", 16)
            for r in range(NR):
                b, hl = divmod(r, 2)
                sl = r % 2
                if r >= 2:
                    sync.wait_ge(pr.sem("b3_pv"), (r - 1) * 4)
                rows = slice(hl * P, (hl + 1) * P)
                for (dst, src) in ((s_Qh, dq[0]), (s_Ql, dq[1]), (s_Kh, dk[0]), (s_Kl, dk[1])):
                    pr.inc(sync.dma_start(
                        out=dst[:, sl, :, :],
                        in_=src[rows, :].rearrange("o (rr t) -> o rr t", t=1024)[:, :, b*256:(b+1)*256]),
                        f"b3_ld{sl}", 16)
                for (dst, src) in ((s_Vh, dvn[0]), (s_Vl, dvn[1])):
                    for rr in range(8):
                        pr.inc(sync.dma_start(
                            out=dst[:, sl, rr, :, :],
                            in_=src.rearrange("(rr bb sh p) o -> p rr bb sh o", p=P, sh=2, bb=4)[:, rr, b, :, rows]),
                            f"b3_ld{sl}", 16)

        @blk.tensor
        def _(t):
            t.wait_ge(pr.sem("b3_mld"), 16)
            for r in range(NR):
                sl = r % 2
                t.wait_ge(pr.sem(f"b3_ld{sl}"), 320 * (r // 2 + 1))
                def scores(qt):
                    g = rqt(r, qt)
                    if g >= 1:
                        t.wait_ge(pr.sem("b3_exp"), g)
                    nkc = qt // 4 + 1
                    qsl = s_Qh[:, sl, qt // 2, (qt % 2)*128:(qt % 2)*128+128]
                    qsl_l = s_Ql[:, sl, qt // 2, (qt % 2)*128:(qt % 2)*128+128]
                    mm = None
                    for kc in range(nkc):
                        ksl_h = s_Kh[:, sl, kc*2:kc*2+2, :].rearrange("p a b -> p (a b)")
                        ksl_l = s_Kl[:, sl, kc*2:kc*2+2, :].rearrange("p a b -> p (a b)")
                        t.matmul(ps_sc[:, kc*512:(kc+1)*512], qsl, ksl_h,
                                 start=True, stop=False)
                        mm = t.matmul(ps_sc[:, kc*512:(kc+1)*512], qsl, ksl_l,
                                      start=False, stop=False)
                    for kc in range(nkc):
                        ksl_h = s_Kh[:, sl, kc*2:kc*2+2, :].rearrange("p a b -> p (a b)")
                        mm = t.matmul(ps_sc[:, kc*512:(kc+1)*512], qsl_l, ksl_h,
                                      start=False, stop=True)
                    pr.inc(mm, "b3_mm")
                def ptrans(qt):
                    g = rqt(r, qt)
                    t.wait_ge(pr.sem("b3_pn"), g + 1)
                    nq = qt // 4 + 1
                    for iq in range(nq):
                        gq = pr.cnt["b3_ptq"] = pr.cnt.get("b3_ptq", 0) + 1
                        if gq >= 3:
                            t.wait_ge(pr.sem("b3_ptlo"), gq - 2)
                        mm = None
                        for j in range(4):
                            kt = iq * 4 + j
                            mm = t.transpose(ps_pt[:, gq % 2, j*128:(j+1)*128],
                                             s_Pn[:, kt*128:(kt+1)*128], s_ident[:, :])
                        pr.inc(mm, "b3_pt")
                def pv(qc):
                    gq = r * 4 + qc
                    t.wait_ge(pr.sem("b3_ptlo"), r * 40 + 2 * (qc + 1) * (qc + 2))
                    if gq >= 1:
                        t.wait_ge(pr.sem("b3_aev"), gq)
                    nkt = qc * 4 + 4
                    mm = None
                    for kt in range(nkt):
                        vh = s_Vh[:, sl, kt // 2, kt % 2, :]
                        vl = s_Vl[:, sl, kt // 2, kt % 2, :]
                        first = kt == 0
                        t.matmul(ps_pv[:, :], vh, s_PTh[:, kt, :], start=first, stop=False)
                        t.matmul(ps_pv[:, :], vh, s_PTl[:, kt, :], start=False, stop=False)
                        mm = t.matmul(ps_pv[:, :], vl, s_PTh[:, kt, :], start=False,
                                      stop=(kt == nkt - 1))
                    pr.inc(mm, "b3_pv")
                for qt in range(16):
                    scores(qt)
                    if qt >= 1:
                        ptrans(qt - 1)
                    if qt >= 1 and (qt - 1) % 4 == 3:
                        pv((qt - 1) // 4)
                ptrans(15)
                pv(3)

        @blk.vector
        def _(v):
            v.wait_ge(pr.sem("b3_mld"), 16)
            for r in range(NR):
                sl = r % 2
                def ptlo(qt):
                    nq = qt // 4 + 1
                    for iq in range(nq):
                        gq = pr.cnt["b3_ptqv"] = pr.cnt.get("b3_ptqv", 0) + 1
                        v.wait_ge(pr.sem("b3_pth"), gq)
                        op = None
                        for j in range(4):
                            kt = iq * 4 + j
                            op = v.tensor_tensor(
                                out=s_PTl[:, kt, (qt % 4)*128:(qt % 4)*128+128],
                                in0=ps_pt[:, gq % 2, j*128:(j+1)*128],
                                in1=s_PTh[:, kt, (qt % 4)*128:(qt % 4)*128+128],
                                op=Alu.subtract)
                        pr.inc(op, "b3_ptlo")
                for qt in range(16):
                    g = rqt(r, qt)
                    nkc = qt // 4 + 1
                    W = nkc * 512
                    v.wait_ge(pr.sem("b3_mm"), g + 1)
                    v.tensor_tensor(out=ps_sc[:, (nkc-1)*512:nkc*512],
                                    in0=ps_sc[:, (nkc-1)*512:nkc*512],
                                    in1=s_mask[:, qt % 4, :], op=Alu.add)
                    F = lambda op: pr.fence(v, op, "b3_hz")
                    F(v.tensor_reduce(out=s_st[:, 0:1], in_=ps_sc[:, 0:W], axis=AX.X,
                                      op=Alu.max))
                    pr.inc(v.tensor_scalar(out=s_st[:, 1:2], in0=s_st[:, 0:1],
                                           scalar1=-C, scalar2=None, op0=Alu.mult),
                           "b3_bias")
                    if qt >= 1:
                        ptlo(qt - 1)
                    v.wait_ge(pr.sem("b3_exp"), g + 1)
                    F(v.reciprocal(out=s_st[:, 2:3], in_=s_st[:, 3:4]))
                    pr.inc(v.tensor_scalar(out=s_Pn[:, 0:W], in0=s_P[:, 0:W],
                                           scalar1=s_st[:, 2:3], scalar2=None,
                                           op0=Alu.mult), "b3_pn")
                ptlo(15)

        @blk.scalar
        def _(sc):
            for r in range(NR):
                def his(qt):
                    nq = qt // 4 + 1
                    for iq in range(nq):
                        gq = pr.cnt["b3_ptqs"] = pr.cnt.get("b3_ptqs", 0) + 1
                        sc.wait_ge(pr.sem("b3_pt"), gq)
                        op = None
                        for j in range(4):
                            kt = iq * 4 + j
                            op = sc.activation(
                                out=s_PTh[:, kt, (qt % 4)*128:(qt % 4)*128+128],
                                in_=ps_pt[:, gq % 2, j*128:(j+1)*128], func=Act.Copy)
                        pr.inc(op, "b3_pth")
                def aev(qc):
                    gq = r * 4 + qc
                    sc.wait_ge(pr.sem("b3_pv"), gq + 1)
                    if gq >= 2:
                        sc.wait_ge(pr.sem("b3_asf"), gq - 1)
                    pr.inc(sc.activation(out=s_at[:, gq % 2, :], in_=ps_pv[:, :],
                                         func=Act.Copy), "b3_aev")
                for qt in range(16):
                    g = rqt(r, qt)
                    nkc = qt // 4 + 1
                    W = nkc * 512
                    sc.wait_ge(pr.sem("b3_bias"), g + 1)
                    pr.inc(sc.activation(out=s_P[:, 0:W], in_=ps_sc[:, 0:W], func=Act.Exp,
                                         bias=s_st[:, 1:2], scale=C,
                                         accum_out=s_st[:, 3:4]), "b3_exp")
                    if qt >= 1:
                        his(qt - 1)
                        if (qt - 1) % 4 == 3:
                            aev((qt - 1) // 4)
                his(15)
                aev(3)

        @blk.gpsimd
        def _(g_):
            for r in range(NR):
                b, hl = divmod(r, 2)
                for qc in range(4):
                    gq = r * 4 + qc
                    g_.wait_ge(pr.sem("b3_aev"), gq + 1)
                    pr.inc(g_.dma_start(out=d_at[hl*P:(hl+1)*P, b, qc*512:(qc+1)*512],
                                        in_=s_at[:, gq % 2, :]), "b3_ast", 16)
                    g_.wait_ge(pr.sem("b3_ast"), pr.val("b3_ast"))
                    g_.sem_inc(pr.sem("b3_asf"), 1)
                    pr.cnt["b3_asf"] += 1



def emit_oproj(pr, g_at, wo_spl, i_bo, i_ones, d_o):
    nc = pr.nc
    with (
        nc.sbuf_tensor("b5_a32", [P, 2, TOK], F32) as s_a32,
        nc.sbuf_tensor("b5_ah", [P, DC, TOK], F16) as s_ah,
        nc.sbuf_tensor("b5_al", [P, DC, TOK], BF16) as s_al,
        nc.sbuf_tensor("b5_wh", [P, DC, 2, 512], F16) as s_wh,
        nc.sbuf_tensor("b5_wl", [P, DC, 2, 512], BF16) as s_wl,
        nc.sbuf_tensor("b5_bo", [P, D], F32) as s_bo,
        nc.sbuf_tensor("b5_bor", [P, D], F32) as s_bor,
        nc.sbuf_tensor("b5_on", [P, P], F32) as s_on,
        nc.sbuf_tensor("b5_o", [P, 2, 2, 512], F32) as s_o,
        nc.psum_tensor("b5_ps", [P, 4, 512], F32) as ps,
        nc.psum_tensor("b5_pb", [P, 512], F32) as ps_b,
        nc.Block() as blk,
    ):
        @blk.sync
        def _(sync):
            pid = sync.partition_id()
            scol = bass.ts(pid, 256)
            pr.inc(sync.dma_start(out=s_bo[0:1, :], in_=i_bo[:, :]), "b5_bld", 16)
            pr.inc(sync.dma_start(out=s_on[0:1, :], in_=i_ones[:, :]), "b5_bld", 16)
            for dc in range(DC):
                rr, oh = divmod(dc, 2)
                if dc >= 2:
                    sync.wait_ge(pr.sem("b5_spl"), dc - 1)
                pr.inc(sync.dma_start(
                    out=s_a32[:, dc % 2, :],
                    in_=g_at[rr, oh*P:(oh+1)*P, :, scol]), f"b5_lda{dc % 2}", 16)

        @blk.vector
        def _(v):
            for dc in range(DC):
                v.wait_ge(pr.sem(f"b5_lda{dc % 2}"), 16 * (dc // 2 + 1))
                v.tensor_copy(out=s_ah[:, dc, :], in_=s_a32[:, dc % 2, :])
                pr.inc(v.tensor_tensor(out=s_al[:, dc, :], in0=s_a32[:, dc % 2, :],
                                       in1=s_ah[:, dc, :], op=Alu.subtract), "b5_spl")
            v.wait_ge(pr.sem("b5_bcp"), 4)
            for half in range(2):
                for tb in range(TB):
                    htb = half * TB + tb
                    v.wait_ge(pr.sem("b5_mm"), htb + 1)
                    if htb >= 2:
                        v.wait_ge(pr.sem("b5_sfin"), htb - 1)
                    op = None
                    for occ in range(2):
                        oc = half * 2 + occ
                        op = v.tensor_tensor(out=s_o[:, htb % 2, occ, :],
                                             in0=ps[:, (tb % 2)*2 + occ, :],
                                             in1=s_bor[:, oc*512:(oc+1)*512], op=Alu.add)
                    pr.inc(op, "b5_oev")

        @blk.scalar
        def _(sc):
            for oc4 in range(4):
                sc.wait_ge(pr.sem("b5_bmm"), oc4 + 1)
                pr.inc(sc.activation(out=s_bor[:, oc4*512:(oc4+1)*512], in_=ps_b[:, :],
                                     func=Act.Copy), "b5_bcp")
            for half in range(2):
                if half == 1:
                    sc.wait_ge(pr.sem("b5_mm"), 8)
                for dc in range(DC):
                    for occ in range(2):
                        oc = half * 2 + occ
                        pr.inc(sc.dma_start_transpose(
                            out=s_wh[:, dc, occ, :],
                            in_=wo_spl[0][oc*512:(oc+1)*512, dc*P:(dc+1)*P]), f"b5_wld{half}", 16)
                        pr.inc(sc.dma_start_transpose(
                            out=s_wl[:, dc, occ, :],
                            in_=wo_spl[1][oc*512:(oc+1)*512, dc*P:(dc+1)*P]), f"b5_wld{half}", 16)

        @blk.tensor
        def _(t):
            t.wait_ge(pr.sem("b5_bld"), 32)
            for oc4 in range(4):
                if oc4 >= 1:
                    t.wait_ge(pr.sem("b5_bcp"), oc4)
                pr.inc(t.matmul(ps_b[:, :], s_on[0:1, :], s_bo[0:1, oc4*512:(oc4+1)*512],
                                start=True, stop=True), "b5_bmm")
            for half in range(2):
                t.wait_ge(pr.sem(f"b5_wld{half}"), pr.val(f"b5_wld{half}"))
                for tb in range(TB):
                    htb = half * TB + tb
                    if htb >= 2:
                        t.wait_ge(pr.sem("b5_oev"), htb - 1)
                    for dc in range(DC):
                        if half == 0 and tb == 0:
                            t.wait_ge(pr.sem("b5_spl"), dc + 1)
                        first, last = dc == 0, dc == DC - 1
                        ah = s_ah[:, dc, tb*P:(tb+1)*P]
                        al = s_al[:, dc, tb*P:(tb+1)*P]
                        mm = None
                        for occ in range(2):
                            wh = s_wh[:, dc, occ, :]
                            wl = s_wl[:, dc, occ, :]
                            pb = (tb % 2) * 2 + occ
                            t.matmul(ps[:, pb, :], ah, wh, start=first, stop=False)
                            t.matmul(ps[:, pb, :], al, wh, start=False, stop=False)
                            mm = t.matmul(ps[:, pb, :], ah, wl, start=False, stop=last)
                        if last:
                            pr.inc(mm, "b5_mm")

        @blk.gpsimd
        def _(g_):
            for half in range(2):
                for tb in range(TB):
                    htb = half * TB + tb
                    g_.wait_ge(pr.sem("b5_oev"), htb + 1)
                    pr.inc(g_.dma_start(
                        out=d_o[tb*P:(tb+1)*P, half*1024:(half+1)*1024],
                        in_=s_o[:, htb % 2, :, :].rearrange("p a b -> p (a b)")), "b5_st", 16)
                    g_.wait_ge(pr.sem("b5_st"), pr.val("b5_st"))
                    g_.sem_inc(pr.sem("b5_sfin"), 1)
                    pr.cnt["b5_sfin"] += 1



def emit_quant(pr, tag, src, dst, dsc, R, C):
    nc = pr.nc
    NT = R // P
    with (
        nc.sbuf_tensor(f"{tag}_x", [P, 2, C], F32) as s_x,
        nc.sbuf_tensor(f"{tag}_v", [P, C], F32) as s_v,
        nc.sbuf_tensor(f"{tag}_sq", [P, C], F32) as s_sq,
        nc.sbuf_tensor(f"{tag}_m", [P, C], U8) as s_m,
        nc.sbuf_tensor(f"{tag}_d8", [P, C], FP8) as s_d8,
        nc.sbuf_tensor(f"{tag}_h8", [P, C], FP8) as s_h8,
        nc.sbuf_tensor(f"{tag}_q2", [P, C], BF16) as s_q2,
        nc.sbuf_tensor(f"{tag}_qb", [P, 2, C], BF16) as s_qb,
        nc.sbuf_tensor(f"{tag}_s", [P, 2, 4], F32) as s_st,
        nc.Block() as blk,
    ):
        @blk.sync
        def _(sync):
            for i in range(NT):
                if i >= 2:
                    sync.wait_ge(pr.sem(f"{tag}_fin"), i - 1)
                pr.inc(sync.dma_start(out=s_x[:, i % 2, :], in_=src[i*P:(i+1)*P, :]),
                       f"{tag}_ld{i % 2}", 16)

        @blk.vector
        def _(v):
            for i in range(NT):
                buf = i % 2
                F = lambda op: pr.fence(v, op, f"{tag}_hz")
                v.wait_ge(pr.sem(f"{tag}_ld{buf}"), 16 * (i // 2 + 1))
                if i >= 2:
                    v.wait_ge(pr.sem(f"{tag}_fin"), i - 1)
                xt = s_x[:, buf, :]
                F(v.tensor_reduce(out=s_st[:, buf, 0:1], in_=xt, axis=AX.X, op=Alu.max,
                                  apply_absolute_value=True))
                F(v.tensor_scalar(out=s_st[:, buf, 1:2], in0=s_st[:, buf, 0:1],
                                  scalar1=1e-30, scalar2=None, op0=Alu.max))
                F(v.reciprocal(out=s_st[:, buf, 2:3], in_=s_st[:, buf, 1:2]))
                F(v.tensor_scalar(out=s_st[:, buf, 2:3], in0=s_st[:, buf, 2:3],
                                  scalar1=448.0, scalar2=None, op0=Alu.mult))
                pr.inc(v.tensor_scalar(out=s_st[:, buf, 3:4], in0=s_st[:, buf, 0:1],
                                       scalar1=1.0/448.0, scalar2=None, op0=Alu.mult),
                       f"{tag}_sc")
                pr.inc(v.tensor_scalar(out=s_v[:, :], in0=xt, scalar1=s_st[:, buf, 2:3],
                                       scalar2=None, op0=Alu.mult), f"{tag}_v")
                v.tensor_tensor(out=s_sq[:, :], in0=s_v[:, :], in1=s_v[:, :], op=Alu.mult)
                v.tensor_scalar(out=s_m[:, :], in0=s_sq[:, :], scalar1=2.0**-8,
                                scalar2=None, op0=Alu.is_ge)
                v.tensor_scalar(out=s_d8[:, :], in0=s_v[:, :], scalar1=-0.0625,
                                scalar2=0.0625, op0=Alu.max, op1=Alu.min)
                v.tensor_scalar(out=s_qb[:, buf, :], in0=s_d8[:, :], scalar1=1.0,
                                scalar2=None, op0=Alu.mult)
                v.wait_ge(pr.sem(f"{tag}_h8"), i + 1)
                v.tensor_scalar(out=s_q2[:, :], in0=s_h8[:, :], scalar1=2.0,
                                scalar2=None, op0=Alu.mult)
                pr.inc(v.copy_predicated(out=s_qb[:, buf, :], mask=s_m[:, :],
                                         data=s_q2[:, :]), f"{tag}_q")

        @blk.scalar
        def _(sc):
            for i in range(NT):
                sc.wait_ge(pr.sem(f"{tag}_v"), i + 1)
                pr.inc(sc.activation(out=s_h8[:, :], in_=s_v[:, :], func=Act.Copy,
                                     scale=0.5), f"{tag}_h8")

        @blk.gpsimd
        def _(g):
            for i in range(NT):
                buf = i % 2
                g.wait_ge(pr.sem(f"{tag}_q"), i + 1)
                g.wait_ge(pr.sem(f"{tag}_sc"), i + 1)
                pr.inc(g.dma_start(out=dst[i*P:(i+1)*P, :], in_=s_qb[:, buf, :]),
                       f"{tag}_st", 16)
                pr.inc(g.dma_start(out=dsc[i*P:(i+1)*P], in_=s_st[:, buf, 3:4]),
                       f"{tag}_st", 16)
                g.wait_ge(pr.sem(f"{tag}_st"), pr.val(f"{tag}_st"))
                g.sem_inc(pr.sem(f"{tag}_fin"), 1)
                pr.cnt[f"{tag}_fin"] += 1


def emit_ffn0_gateup(pr, d_qx, d_sx, qwg_flat, qwu_flat, d_wsrow, d_t):
    nc = pr.nc
    IC0 = I0 // 512
    with (
        nc.sbuf_tensor("c3_qxT", [P, DC, TB, P], BF16) as s_qxT,
        nc.sbuf_tensor("c3_w", [P, 2, DC, 2, 512], BF16) as s_w,
        nc.sbuf_tensor("c3_wsc", [P, 2, 2, 512], F32) as s_wsc,
        nc.sbuf_tensor("c3_sx", [P, TB], F32) as s_sx,
        nc.sbuf_tensor("c3_sg", [P, 2, 512], F32) as s_sg,
        nc.sbuf_tensor("c3_sg2", [P, 2, 512], F32) as s_sg2,
        nc.sbuf_tensor("c3_t", [P, 2, 2, 512], F32) as s_t,
        nc.psum_tensor("c3_ps", [P, 8, 512], F32) as ps,
        nc.Block() as blk,
    ):
        NST = IC0 * 4

        @blk.sync
        def _(sync):
            pr.inc(sync.dma_start(out=s_sx[:, :],
                                  in_=d_sx.rearrange("(t p) -> p t", p=P),
                                  allow_slow_non_contiguous=True), "c3_sxl", 16)
            for dc in range(DC):
                for tb in range(TB):
                    pr.inc(sync.dma_start_transpose(
                        out=s_qxT[:, dc, tb, :],
                        in_=d_qx[tb*P:(tb+1)*P, dc*P:(dc+1)*P]), "c3_pl", 16)
            for ic in range(IC0):
                if ic >= 2:
                    sync.wait_ge(pr.sem("c3_t"), (ic - 1) * 4)
                pr.inc(sync.dma_start(
                    out=s_wsc[:, ic % 2, 0, :],
                    in_=d_wsrow[:, ic*512:(ic+1)*512].broadcast_to([P, 512])),
                    f"c3_wl{ic % 2}", 16)
                pr.inc(sync.dma_start(
                    out=s_wsc[:, ic % 2, 1, :],
                    in_=d_wsrow[:, 6144+ic*512:6144+(ic+1)*512].broadcast_to([P, 512])),
                    f"c3_wl{ic % 2}", 16)
                for dc in range(DC):
                    pr.inc(sync.dma_start_transpose(
                        out=s_w[:, ic % 2, dc, 0, :],
                        in_=qwg_flat[ic*512:(ic+1)*512, dc*P:(dc+1)*P]), f"c3_wl{ic % 2}", 16)
                    pr.inc(sync.dma_start_transpose(
                        out=s_w[:, ic % 2, dc, 1, :],
                        in_=qwu_flat[ic*512:(ic+1)*512, dc*P:(dc+1)*P]), f"c3_wl{ic % 2}", 16)

        @blk.tensor
        def _(t):
            t.wait_ge(pr.sem("c3_pl"), 16 * (DC * TB))
            for ic in range(IC0):
                t.wait_ge(pr.sem(f"c3_wl{ic % 2}"), 16 * 34 * (ic // 2 + 1))
                for tp in range(4):
                    g = ic * 4 + tp
                    st = g % 2
                    if g >= 2:
                        t.wait_ge(pr.sem("c3_t"), g - 1)
                    mm = None
                    for dc in range(DC):
                        first, last = dc == 0, dc == DC - 1
                        for mi in range(2):
                            for tq in range(2):
                                tb = tp * 2 + tq
                                mm = t.matmul(ps[:, st*4 + mi*2 + tq, :],
                                              s_qxT[:, dc, tb, :],
                                              s_w[:, ic % 2, dc, mi, :],
                                              start=first, stop=last)
                    pr.inc(mm, "c3_mm")

        @blk.vector
        def _(v):
            v.wait_ge(pr.sem("c3_sxl"), 16)
            for ic in range(IC0):
                for tp in range(4):
                    g = ic * 4 + tp
                    st = g % 2
                    v.wait_ge(pr.sem("c3_mm"), g + 1)
                    if g >= 2:
                        v.wait_ge(pr.sem("c3_tf"), g - 1)
                    for tq in range(2):
                        pr.inc(v.tensor_tensor(out=s_sg[:, tq, :], in0=ps[:, st*4 + tq, :],
                                               in1=s_wsc[:, ic % 2, 0, :], op=Alu.mult),
                               "c3_eg")
                    for tq in range(2):
                        tb = tp * 2 + tq
                        v.wait_ge(pr.sem("c3_sg"), g * 2 + tq + 1)
                        v.tensor_tensor(out=s_t[:, st, tq, :], in0=ps[:, st*4 + 2 + tq, :],
                                        in1=s_wsc[:, ic % 2, 1, :], op=Alu.mult)
                        v.tensor_scalar(out=s_t[:, st, tq, :], in0=s_t[:, st, tq, :],
                                        scalar1=s_sx[:, tb:tb+1], scalar2=None,
                                        op0=Alu.mult)
                        op = v.tensor_tensor(out=s_t[:, st, tq, :], in0=s_t[:, st, tq, :],
                                             in1=s_sg2[:, tq, :], op=Alu.mult)
                        if tq == 1:
                            pr.inc(op, "c3_t")

        @blk.scalar
        def _(sc):
            sc.wait_ge(pr.sem("c3_sxl"), 16)
            for ic in range(IC0):
                for tp in range(4):
                    g = ic * 4 + tp
                    st = g % 2
                    for tq in range(2):
                        tb = tp * 2 + tq
                        sc.wait_ge(pr.sem("c3_eg"), g * 2 + tq + 1)
                        pr.inc(sc.activation(out=s_sg2[:, tq, :], in_=s_sg[:, tq, :],
                                             func=Act.Silu, scale=s_sx[:, tb:tb+1]),
                               "c3_sg")

        @blk.gpsimd
        def _(g_):
            for ic in range(IC0):
                for tp in range(4):
                    g = ic * 4 + tp
                    st = g % 2
                    g_.wait_ge(pr.sem("c3_t"), g + 1)
                    for tq in range(2):
                        tb = tp * 2 + tq
                        pr.inc(g_.dma_start(out=d_t[tb*P:(tb+1)*P, ic*512:(ic+1)*512],
                                            in_=s_t[:, st, tq, :]), "c3_tst", 16)
                    g_.wait_ge(pr.sem("c3_tst"), pr.val("c3_tst"))
                    g_.sem_inc(pr.sem("c3_tf"), 1)
                    pr.cnt["c3_tf"] += 1


def emit_ffn0_down(pr, d_qt, d_st, qwd_flat, d_wsrow, d_y0):
    nc = pr.nc
    ICD = I0 // P
    with (
        nc.sbuf_tensor("c5_qtT", [P, ICD, TB, P], BF16) as s_qtT,
        nc.sbuf_tensor("c5_w", [P, 8, 512], BF16) as s_w,
        nc.sbuf_tensor("c5_wsc", [P, 2, 512], F32) as s_wsc,
        nc.sbuf_tensor("c5_st", [P, TB], F32) as s_stc,
        nc.sbuf_tensor("c5_y", [P, 2, 4, 512], F32) as s_y,
        nc.psum_tensor("c5_ps", [P, 8, 512], F32) as ps,
        nc.Block() as blk,
    ):
        @blk.sync
        def _(sync):
            pr.inc(sync.dma_start(out=s_stc[:, :],
                                  in_=d_st.rearrange("(t p) -> p t", p=P),
                                  allow_slow_non_contiguous=True), "c5_stl", 16)
            for ic in range(ICD):
                for tb in range(TB):
                    pr.inc(sync.dma_start_transpose(
                        out=s_qtT[:, ic, tb, :],
                        in_=d_qt[tb*P:(tb+1)*P, ic*P:(ic+1)*P]), "c5_pl", 16)
            for hc in range(4):
                if hc >= 2:
                    sync.wait_ge(pr.sem("c5_y"), 2 * hc - 2)
                pr.inc(sync.dma_start(
                    out=s_wsc[:, hc % 2, :],
                    in_=d_wsrow[:, 12288+hc*512:12288+(hc+1)*512].broadcast_to([P, 512])),
                    f"c5_ws{hc % 2}", 16)
                for half in range(2):
                    g = hc * 2 + half
                    for ic in range(ICD):
                        n = g * ICD + ic
                        buf = n % 8
                        if n >= 8:
                            sync.wait_ge(pr.sem("c5_ic"), n - 7)
                        pr.inc(sync.dma_start_transpose(
                            out=s_w[:, buf, :],
                            in_=qwd_flat[hc*512:(hc+1)*512, ic*P:(ic+1)*P]),
                            f"c5_wl{buf}", 16)

        @blk.tensor
        def _(t):
            t.wait_ge(pr.sem("c5_pl"), 16 * (ICD * TB))
            for hc in range(4):
                for half in range(2):
                    g = hc * 2 + half
                    st = g % 2
                    if g >= 2:
                        t.wait_ge(pr.sem("c5_y"), g - 1)
                    for ic in range(ICD):
                        n = g * ICD + ic
                        buf = n % 8
                        t.wait_ge(pr.sem(f"c5_wl{buf}"), 16 * (n // 8 + 1))
                        first, last = ic == 0, ic == ICD - 1
                        mm = None
                        for tq in range(4):
                            tb = half * 4 + tq
                            mm = t.matmul(ps[:, st*4 + tq, :], s_qtT[:, ic, tb, :],
                                          s_w[:, buf, :], start=first, stop=last)
                        pr.inc(mm, "c5_ic")

        @blk.vector
        def _(v):
            v.wait_ge(pr.sem("c5_stl"), 16)
            for hc in range(4):
                for half in range(2):
                    g = hc * 2 + half
                    st = g % 2
                    v.wait_ge(pr.sem("c5_ic"), (g + 1) * ICD)
                    v.wait_ge(pr.sem(f"c5_ws{hc % 2}"), 16 * (hc // 2 + 1))
                    if g >= 2:
                        v.wait_ge(pr.sem("c5_yf"), g - 1)
                    op = None
                    for tq in range(4):
                        tb = half * 4 + tq
                        v.tensor_tensor(out=s_y[:, st, tq, :], in0=ps[:, st*4 + tq, :],
                                        in1=s_wsc[:, hc % 2, :], op=Alu.mult)
                        op = v.tensor_scalar(out=s_y[:, st, tq, :], in0=s_y[:, st, tq, :],
                                             scalar1=s_stc[:, tb:tb+1], scalar2=None,
                                             op0=Alu.mult)
                    pr.inc(op, "c5_y")

        @blk.gpsimd
        def _(g_):
            for hc in range(4):
                for half in range(2):
                    g = hc * 2 + half
                    st = g % 2
                    g_.wait_ge(pr.sem("c5_y"), g + 1)
                    for tq in range(4):
                        tb = half * 4 + tq
                        pr.inc(g_.dma_start(out=d_y0[tb*P:(tb+1)*P, hc*512:(hc+1)*512],
                                            in_=s_y[:, st, tq, :]), "c5_yst", 16)
                    g_.wait_ge(pr.sem("c5_yst"), pr.val("c5_yst"))
                    g_.sem_inc(pr.sem("c5_yf"), 1)
                    pr.cnt["c5_yf"] += 1



def emit_resid3(pr, tag, src_x, src_a, src_b, dst):
    nc = pr.nc
    with (
        nc.sbuf_tensor(f"{tag}_x", [P, 2, D], F32) as s_x,
        nc.sbuf_tensor(f"{tag}_a", [P, 2, D], F32) as s_a,
        nc.sbuf_tensor(f"{tag}_b", [P, 2, D], F32) as s_b,
        nc.sbuf_tensor(f"{tag}_o", [P, 2, D], F32) as s_o,
        nc.Block() as blk,
    ):
        @blk.sync
        def _(sync):
            for tb in range(TB):
                if tb >= 2:
                    sync.wait_ge(pr.sem(f"{tag}_fin"), tb - 1)
                if len(src_x.shape) == 3:
                    bb, sh = divmod(tb, 2)
                    xin = src_x[bb, sh*P:(sh+1)*P, :]
                else:
                    xin = src_x[tb*P:(tb+1)*P, :]
                pr.inc(sync.dma_start(out=s_x[:, tb % 2, :], in_=xin), f"{tag}_l{tb % 2}", 16)
                pr.inc(sync.dma_start(out=s_a[:, tb % 2, :],
                                      in_=src_a[tb*P:(tb+1)*P, :]), f"{tag}_l{tb % 2}", 16)
                pr.inc(sync.dma_start(out=s_b[:, tb % 2, :],
                                      in_=src_b[tb*P:(tb+1)*P, :]), f"{tag}_l{tb % 2}", 16)

        @blk.vector
        def _(v):
            for tb in range(TB):
                buf = tb % 2
                v.wait_ge(pr.sem(f"{tag}_l{buf}"), 48 * (tb // 2 + 1))
                if tb >= 2:
                    v.wait_ge(pr.sem(f"{tag}_fin"), tb - 1)
                v.tensor_tensor(out=s_o[:, buf, :], in0=s_x[:, buf, :],
                                in1=s_a[:, buf, :], op=Alu.add)
                pr.inc(v.tensor_tensor(out=s_o[:, buf, :], in0=s_o[:, buf, :],
                                       in1=s_b[:, buf, :], op=Alu.add), f"{tag}_s")

        @blk.gpsimd
        def _(g):
            for tb in range(TB):
                g.wait_ge(pr.sem(f"{tag}_s"), tb + 1)
                if len(dst.shape) == 3:
                    bb, sh = divmod(tb, 2)
                    dsl = dst[bb, sh*P:(sh+1)*P, :]
                else:
                    dsl = dst[tb*P:(tb+1)*P, :]
                pr.inc(g.dma_start(out=dsl, in_=s_o[:, tb % 2, :]),
                       f"{tag}_st", 16)
                g.wait_ge(pr.sem(f"{tag}_st"), pr.val(f"{tag}_st"))
                g.sem_inc(pr.sem(f"{tag}_fin"), 1)
                pr.cnt[f"{tag}_fin"] += 1



def emit_s4d_k(pr, ins, d_k):
    nc = pr.nc
    SH3, SH2 = [P, DC, 8], [P, DC]
    with (
        nc.sbuf_tensor("e2_in", [P, DC, 8 * 4], F32) as s_in,
        nc.sbuf_tensor("e2_ldt", [P, DC], F32) as s_ldt,
        nc.sbuf_tensor("e2_w", [P, DC, 8 * 12], F32) as s_w,
        nc.sbuf_tensor("e2_k", [P, DC, 4], F32) as s_k,
        nc.sbuf_tensor("e2_kt", [DC, P], F32) as s_kt,
        nc.sbuf_tensor("e2_ident", [P, P], F32) as s_id2,
        nc.sbuf_tensor("e2_hpi", [P, 1], F32) as s_hpi,
        nc.psum_tensor("e2_ps", [P, 128], F32) as ps,
        nc.Block() as blk,
    ):
        L = {n: s_in[:, :, i*8:(i+1)*8] for i, n in enumerate(("arl", "aim", "cre", "cim"))}
        W = lambda i: s_w[:, :, i*8:(i+1)*8]

        @blk.sync
        def _(sync):
            for i, n in enumerate(("arl", "aim", "cre", "cim")):
                pr.inc(sync.dma_start(out=L[n], in_=ins[n][:, :, :]), "e2_ld", 16)
            pr.inc(sync.dma_start(out=s_ldt[:, :], in_=ins["ldt"][:, :]), "e2_ld", 16)
            pr.inc(sync.dma_start(out=s_id2[:, :], in_=ins["ident"][:, :]), "e2_ld", 16)

        @blk.scalar
        def _(sc):
            sc.wait_ge(pr.sem("e2_ld"), 96)
            sc_f = pr.fence(sc, sc.memzero(s_hpi[:, :]), "e2_shz") if False else None
            pr.inc(sc.activation(out=s_ldt[:, :], in_=s_ldt[:, :], func=Act.Exp), "e2_dt")
            sc.wait_ge(pr.sem("e2_dt8"), 1)
            pr.inc(sc.activation(out=W(1), in_=L["arl"], func=Act.Exp), "e2_are")
            sc.wait_ge(pr.sem("e2_dta"), 1)
            sc.wait_ge(pr.sem("e2_hpi"), 1)
            sc.activation(out=W(4), in_=W(2), func=Act.Exp)
            sc.activation(out=W(5), in_=W(3), func=Act.Sin, scale=-1.0,
                          bias=s_hpi[:, 0:1])
            pr.inc(sc.activation(out=W(6), in_=W(3), func=Act.Sin), "e2_trig")

        @blk.vector
        def _(v):
            F = lambda op: pr.fence(v, op, "e2_hz")
            pr.inc(v.memset(s_hpi[:, :], float(np.pi / 2)), "e2_hpi")
            v.wait_ge(pr.sem("e2_dt"), 1)
            for n in range(8):
                op = v.tensor_copy(out=s_w[:, :, 0*8+n:0*8+n+1], in_=s_ldt[:, :].unsqueeze(2))
            F(op)
            pr.inc(v.engine_nop(), "e2_dt8")
            v.wait_ge(pr.sem("e2_are"), 1)
            F(v.tensor_scalar(out=W(1), in0=W(1), scalar1=-1.0, scalar2=None, op0=Alu.mult))
            v.tensor_tensor(out=W(2), in0=W(1), in1=W(0), op=Alu.mult)
            F(v.tensor_tensor(out=W(3), in0=L["aim"], in1=W(0), op=Alu.mult))
            pr.inc(v.engine_nop(), "e2_dta")
            v.wait_ge(pr.sem("e2_trig"), 1)
            v.tensor_tensor(out=W(7), in0=W(4), in1=W(5), op=Alu.mult)
            F(v.tensor_tensor(out=W(8), in0=W(4), in1=W(6), op=Alu.mult))
            F(v.tensor_scalar(out=W(4), in0=W(7), scalar1=-1.0, scalar2=None, op0=Alu.add))
            v.tensor_tensor(out=W(5), in0=W(1), in1=W(1), op=Alu.mult)
            v.tensor_tensor(out=W(6), in0=L["aim"], in1=L["aim"], op=Alu.mult)
            F(v.tensor_tensor(out=W(5), in0=W(5), in1=W(6), op=Alu.add))
            F(v.reciprocal(out=W(5), in_=W(5)))
            v.tensor_tensor(out=W(9), in0=L["cre"], in1=W(4), op=Alu.mult)
            F(v.tensor_tensor(out=W(6), in0=L["cim"], in1=W(8), op=Alu.mult))
            F(v.tensor_tensor(out=W(9), in0=W(9), in1=W(6), op=Alu.subtract))
            v.tensor_tensor(out=W(10), in0=L["cre"], in1=W(8), op=Alu.mult)
            F(v.tensor_tensor(out=W(6), in0=L["cim"], in1=W(4), op=Alu.mult))
            F(v.tensor_tensor(out=W(10), in0=W(10), in1=W(6), op=Alu.add))
            v.tensor_tensor(out=W(11), in0=W(9), in1=W(1), op=Alu.mult)
            F(v.tensor_tensor(out=W(6), in0=W(10), in1=L["aim"], op=Alu.mult))
            F(v.tensor_tensor(out=W(11), in0=W(11), in1=W(6), op=Alu.add))
            F(v.tensor_tensor(out=W(11), in0=W(11), in1=W(5), op=Alu.mult))
            v.tensor_tensor(out=W(6), in0=W(10), in1=W(1), op=Alu.mult)
            F(v.tensor_tensor(out=W(10), in0=W(9), in1=L["aim"], op=Alu.mult))
            F(v.tensor_tensor(out=W(6), in0=W(6), in1=W(10), op=Alu.subtract))
            F(v.tensor_tensor(out=W(6), in0=W(6), in1=W(5), op=Alu.mult))
            F(v.tensor_reduce(out=s_k[:, :, 0:1], in_=W(11), axis=AX.X, op=Alu.add))
            F(v.tensor_copy(out=W(0), in_=W(7)))
            F(v.tensor_copy(out=W(1), in_=W(8)))
            for mm_ in range(1, 4):
                v.tensor_tensor(out=W(9), in0=W(11), in1=W(7), op=Alu.mult)
                F(v.tensor_tensor(out=W(10), in0=W(6), in1=W(8), op=Alu.mult))
                F(v.tensor_tensor(out=W(9), in0=W(9), in1=W(10), op=Alu.subtract))
                F(v.tensor_reduce(out=s_k[:, :, mm_:mm_+1], in_=W(9), axis=AX.X, op=Alu.add))
                if mm_ < 3:
                    v.tensor_tensor(out=W(9), in0=W(7), in1=W(0), op=Alu.mult)
                    F(v.tensor_tensor(out=W(10), in0=W(8), in1=W(1), op=Alu.mult))
                    v.tensor_tensor(out=W(2), in0=W(7), in1=W(1), op=Alu.mult)
                    F(v.tensor_tensor(out=W(3), in0=W(8), in1=W(0), op=Alu.mult))
                    F(v.tensor_tensor(out=W(7), in0=W(9), in1=W(10), op=Alu.subtract))
                    F(v.tensor_tensor(out=W(8), in0=W(2), in1=W(3), op=Alu.add))
            pr.inc(v.tensor_scalar(out=s_k[:, :, :], in0=s_k[:, :, :], scalar1=2.0,
                                   scalar2=None, op0=Alu.mult), "e2_k")
            for mm_ in range(4):
                v.wait_ge(pr.sem("e2_ktr"), mm_ + 1)
                if mm_ >= 1:
                    v.wait_ge(pr.sem("e2_kst"), mm_)
                pr.inc(v.tensor_copy(out=s_kt[:, :], in_=ps[0:DC, 0:P]), "e2_kev")

        @blk.tensor
        def _(t):
            t.wait_ge(pr.sem("e2_k"), 1)
            for mm_ in range(4):
                if mm_ >= 1:
                    t.wait_ge(pr.sem("e2_kst"), mm_)
                pr.inc(t.transpose(ps[0:DC, 0:P], s_k[:, :, mm_:mm_+1].squeeze(2), s_id2[:, :]),
                       "e2_ktr")

        @blk.gpsimd
        def _(g):
            for mm_ in range(4):
                g.wait_ge(pr.sem("e2_kev"), mm_ + 1)
                pr.inc(g.dma_start(out=d_k[mm_, :].rearrange("(c p) -> c p", p=P),
                                   in_=s_kt[:, :]), "e2_kstd", 16)
                g.wait_ge(pr.sem("e2_kstd"), pr.val("e2_kstd"))
                g.sem_inc(pr.sem("e2_kst"), 1)
                pr.cnt["e2_kst"] += 1


def emit_mix(pr, d_xn1, d_k, i_dpar, d_mix):
    nc = pr.nc
    with (
        nc.sbuf_tensor("e3_kbc", [P, 4, D], F32) as s_kbc,
        nc.sbuf_tensor("e3_db", [P, D], F32) as s_db,
        nc.sbuf_tensor("e3_x", [P, 2, 4, D], F32) as s_x,
        nc.sbuf_tensor("e3_o", [P, 2, 4, D], F32) as s_o,
        nc.sbuf_tensor("e3_tmp", [P, D], F32) as s_tmp,
        nc.Block() as blk,
    ):
        @blk.sync
        def _(sync):
            for mm_ in range(4):
                pr.inc(sync.dma_start(out=s_kbc[:, mm_, :],
                                      in_=d_k[mm_:mm_+1, :].broadcast_to([P, D])),
                       "e3_kld", 16)
            pr.inc(sync.dma_start(out=s_db[:, :],
                                  in_=i_dpar[:, :].broadcast_to([P, D])), "e3_kld", 16)
            for sh in range(2):
                if sh >= 1:
                    pass
                for b in range(4):
                    pr.inc(sync.dma_start(out=s_x[:, sh, b, :],
                                          in_=d_xn1[(b*2 + sh)*P:(b*2 + sh + 1)*P, :]),
                           f"e3_x{sh}", 16)

        @blk.vector
        def _(v):
            v.wait_ge(pr.sem("e3_kld"), 80)
            f = pr.fence(v, v.tensor_tensor(out=s_kbc[:, 0, :], in0=s_kbc[:, 0, :],
                                            in1=s_db[:, :], op=Alu.add), "e3_hz")
            for sh in range(2):
                v.wait_ge(pr.sem(f"e3_x{sh}"), 64)
                for b in range(4):
                    v.tensor_tensor(out=s_o[:, sh, b, :], in0=s_kbc[:, 0, :],
                                    in1=s_x[:, sh, b, :], op=Alu.mult)
                    for mm_ in range(1, b + 1):
                        v.tensor_tensor(out=s_tmp[:, :], in0=s_kbc[:, mm_, :],
                                        in1=s_x[:, sh, b - mm_, :], op=Alu.mult)
                        v.tensor_tensor(out=s_o[:, sh, b, :], in0=s_o[:, sh, b, :],
                                        in1=s_tmp[:, :], op=Alu.add)
                pr.inc(v.engine_nop(), "e3_mx")

        @blk.gpsimd
        def _(g):
            for sh in range(2):
                g.wait_ge(pr.sem("e3_mx"), sh + 1)
                for b in range(4):
                    pr.inc(g.dma_start(out=d_mix[(b*2 + sh)*P:(b*2 + sh + 1)*P, :],
                                       in_=s_o[:, sh, b, :]), "e3_st", 16)
            g.wait_ge(pr.sem("e3_st"), pr.val("e3_st"))



def emit_recon(pr, taps, bks, outs):
    nc = pr.nc
    RSH = I1 * D // 8
    NT = RSH // 65536
    with (
        nc.sbuf_tensor("f1_tp", [P, 2, 4, 512], F32) as s_tp,
        nc.sbuf_tensor("f1_bk", [P, 2, 4, 512], F32) as s_bk,
        nc.sbuf_tensor("f1_w", [P, 2, 512], F32) as s_w,
        nc.sbuf_tensor("f1_a", [P, 512], F32) as s_a,
        nc.sbuf_tensor("f1_h", [P, 2, 512], F16) as s_h,
        nc.sbuf_tensor("f1_l", [P, 2, 512], BF16) as s_l,
        nc.Block() as blk,
    ):
        jobs = [(mi, ti) for mi in range(3) for ti in range(NT)]

        @blk.sync
        def _(sync):
            for j, (mi, ti) in enumerate(jobs):
                buf = j % 2
                if j >= 2:
                    sync.wait_ge(pr.sem("f1_fin"), j - 1)
                sl = slice(ti * 65536, (ti + 1) * 65536)
                for k in range(4):
                    pr.inc(sync.dma_start(
                        out=s_tp[:, buf, k, :],
                        in_=taps[mi][k][sl].rearrange("(p f) -> p f", f=512)),
                        f"f1_ld{buf}", 16)
                    pr.inc(sync.dma_start(
                        out=s_bk[:, buf, k, :],
                        in_=bks[k][sl].rearrange("(p f) -> p f", f=512)),
                        f"f1_ld{buf}", 16)

        @blk.vector
        def _(v):
            for j, (mi, ti) in enumerate(jobs):
                buf = j % 2
                v.wait_ge(pr.sem(f"f1_ld{buf}"), 128 * (j // 2 + 1))
                if j >= 2:
                    v.wait_ge(pr.sem("f1_fin"), j - 1)
                v.tensor_tensor(out=s_w[:, buf, :], in0=s_bk[:, buf, 0, :],
                                in1=s_tp[:, buf, 0, :], op=Alu.mult)
                for k in range(1, 4):
                    v.tensor_tensor(out=s_a[:, :], in0=s_bk[:, buf, k, :],
                                    in1=s_tp[:, buf, k, :], op=Alu.mult)
                    v.tensor_tensor(out=s_w[:, buf, :], in0=s_w[:, buf, :],
                                    in1=s_a[:, :], op=Alu.add)
                v.tensor_copy(out=s_h[:, buf, :], in_=s_w[:, buf, :])
                pr.inc(v.tensor_tensor(out=s_l[:, buf, :], in0=s_w[:, buf, :],
                                       in1=s_h[:, buf, :], op=Alu.subtract), "f1_sp")

        @blk.gpsimd
        def _(g):
            for j, (mi, ti) in enumerate(jobs):
                buf = j % 2
                g.wait_ge(pr.sem("f1_sp"), j + 1)
                sl = slice(ti * 65536, (ti + 1) * 65536)
                pr.inc(g.dma_start(out=outs[mi][0][sl].rearrange("(p f) -> p f", f=512),
                                   in_=s_h[:, buf, :]), "f1_st", 16)
                pr.inc(g.dma_start(out=outs[mi][1][sl].rearrange("(p f) -> p f", f=512),
                                   in_=s_l[:, buf, :]), "f1_st", 16)
                g.wait_ge(pr.sem("f1_st"), pr.val("f1_st"))
                g.sem_inc(pr.sem("f1_fin"), 1)
                pr.cnt["f1_fin"] += 1


def emit_ffn1_gu(pr, d_xn1t, w1g, w1u, d_t1):
    nc = pr.nc
    IC1 = I1 // 512
    with (
        nc.sbuf_tensor("f2_xh", [P, DC, TB, P], F16) as s_xh,
        nc.sbuf_tensor("f2_xl", [P, DC, TB, P], BF16) as s_xl,
        nc.sbuf_tensor("f2_wh", [P, 2, DC, 2, 512], F16) as s_wh,
        nc.sbuf_tensor("f2_wl", [P, 2, DC, 2, 512], BF16) as s_wl,
        nc.sbuf_tensor("f2_sg", [P, 2, 512], F32) as s_sg2,
        nc.sbuf_tensor("f2_t", [P, 512], F32) as s_t,
        nc.sbuf_tensor("f2_th", [P, 2, 2, 512], F16) as s_th,
        nc.sbuf_tensor("f2_tl", [P, 2, 2, 512], BF16) as s_tl,
        nc.psum_tensor("f2_ps", [P, 8, 512], F32) as ps,
        nc.Block() as blk,
    ):
        @blk.sync
        def _(sync):
            for dc in range(DC):
                for tb in range(TB):
                    pr.inc(sync.dma_start(
                        out=s_xh[:, dc, tb, :],
                        in_=d_xn1t[0][dc*P:(dc+1)*P, tb*P:(tb+1)*P]), "f2_pl", 16)
                    pr.inc(sync.dma_start(
                        out=s_xl[:, dc, tb, :],
                        in_=d_xn1t[1][dc*P:(dc+1)*P, tb*P:(tb+1)*P]), "f2_pl", 16)
            for ic in range(IC1):
                if ic >= 2:
                    sync.wait_ge(pr.sem("f2_t"), (ic - 1) * 4)
                for dc in range(DC):
                    for mi, wsp in enumerate((w1g, w1u)):
                        pr.inc(sync.dma_start_transpose(
                            out=s_wh[:, ic % 2, dc, mi, :],
                            in_=wsp[0][ic*512:(ic+1)*512, dc*P:(dc+1)*P]),
                            f"f2_wl{ic % 2}", 16)
                        pr.inc(sync.dma_start_transpose(
                            out=s_wl[:, ic % 2, dc, mi, :],
                            in_=wsp[1][ic*512:(ic+1)*512, dc*P:(dc+1)*P]),
                            f"f2_wl{ic % 2}", 16)

        @blk.tensor
        def _(t):
            t.wait_ge(pr.sem("f2_pl"), 16 * DC * TB * 2)
            for ic in range(IC1):
                t.wait_ge(pr.sem(f"f2_wl{ic % 2}"), 16 * 64 * (ic // 2 + 1))
                for tp in range(4):
                    g = ic * 4 + tp
                    st = g % 2
                    if g >= 2:
                        t.wait_ge(pr.sem("f2_t"), g - 1)
                    mm = None
                    for dc in range(DC):
                        first, last = dc == 0, dc == DC - 1
                        for mi in range(2):
                            for tq in range(2):
                                tb = tp * 2 + tq
                                pb = st * 4 + mi * 2 + tq
                                xh = s_xh[:, dc, tb, :]
                                xl = s_xl[:, dc, tb, :]
                                wh = s_wh[:, ic % 2, dc, mi, :]
                                wl = s_wl[:, ic % 2, dc, mi, :]
                                t.matmul(ps[:, pb, :], xh, wh, start=first, stop=False)
                                t.matmul(ps[:, pb, :], xl, wh, start=False, stop=False)
                                mm = t.matmul(ps[:, pb, :], xh, wl, start=False, stop=last)
                    pr.inc(mm, "f2_mm")

        @blk.scalar
        def _(sc):
            for ic in range(IC1):
                for tp in range(4):
                    g = ic * 4 + tp
                    st = g % 2
                    sc.wait_ge(pr.sem("f2_mm"), g + 1)
                    if g >= 1:
                        sc.wait_ge(pr.sem("f2_t"), g)
                    op = None
                    for tq in range(2):
                        op = sc.activation(out=s_sg2[:, tq, :], in_=ps[:, st*4 + tq, :],
                                           func=Act.Silu)
                    pr.inc(op, "f2_sg")

        @blk.vector
        def _(v):
            for ic in range(IC1):
                for tp in range(4):
                    g = ic * 4 + tp
                    st = g % 2
                    v.wait_ge(pr.sem("f2_sg"), g + 1)
                    if g >= 2:
                        v.wait_ge(pr.sem("f2_tf"), g - 1)
                    op = None
                    for tq in range(2):
                        v.tensor_tensor(out=s_t[:, :], in0=ps[:, st*4 + 2 + tq, :],
                                        in1=s_sg2[:, tq, :], op=Alu.mult)
                        v.tensor_copy(out=s_th[:, st, tq, :], in_=s_t[:, :])
                        op = v.tensor_tensor(out=s_tl[:, st, tq, :], in0=s_t[:, :],
                                             in1=s_th[:, st, tq, :], op=Alu.subtract)
                    pr.inc(op, "f2_t")

        @blk.gpsimd
        def _(g_):
            for ic in range(IC1):
                for tp in range(4):
                    g = ic * 4 + tp
                    st = g % 2
                    g_.wait_ge(pr.sem("f2_t"), g + 1)
                    for tq in range(2):
                        tb = tp * 2 + tq
                        pr.inc(g_.dma_start(out=d_t1[0][tb*P:(tb+1)*P, ic*512:(ic+1)*512],
                                            in_=s_th[:, st, tq, :]), "f2_ts", 16)
                        pr.inc(g_.dma_start(out=d_t1[1][tb*P:(tb+1)*P, ic*512:(ic+1)*512],
                                            in_=s_tl[:, st, tq, :]), "f2_ts", 16)
                    g_.wait_ge(pr.sem("f2_ts"), pr.val("f2_ts"))
                    g_.sem_inc(pr.sem("f2_tf"), 1)
                    pr.cnt["f2_tf"] += 1


def emit_ffn1_down(pr, d_t1, w1d, d_y1):
    nc = pr.nc
    ICD = I1 // P
    with (
        nc.sbuf_tensor("f3_sh", [P, 4, TOK], F16) as s_sh,
        nc.sbuf_tensor("f3_sl", [P, 4, TOK], BF16) as s_sl,
        nc.sbuf_tensor("f3_wh", [P, 4, 512], F16) as s_wwh,
        nc.sbuf_tensor("f3_wl", [P, 4, 512], BF16) as s_wwl,
        nc.sbuf_tensor("f3_y", [P, 2, 8, 512], F32) as s_y,
        nc.psum_tensor("f3_ps", [P, 8, 512], F32) as ps,
        nc.Block() as blk,
    ):
        @blk.sync
        def _(sync):
            for hc in range(4):
                for ic in range(ICD):
                    n = hc * ICD + ic
                    buf = n % 4
                    if n >= 4:
                        sync.wait_ge(pr.sem("f3_ic"), n - 3)
                    pr.inc(sync.dma_start_transpose(
                        out=s_sh[:, buf, :], in_=d_t1[0][:, ic*P:(ic+1)*P]),
                        f"f3_wl{buf}", 16)
                    pr.inc(sync.dma_start_transpose(
                        out=s_sl[:, buf, :], in_=d_t1[1][:, ic*P:(ic+1)*P]),
                        f"f3_wl{buf}", 16)
                    pr.inc(sync.dma_start_transpose(
                        out=s_wwh[:, buf, :], in_=w1d[0][hc*512:(hc+1)*512, ic*P:(ic+1)*P]),
                        f"f3_wl{buf}", 16)
                    pr.inc(sync.dma_start_transpose(
                        out=s_wwl[:, buf, :], in_=w1d[1][hc*512:(hc+1)*512, ic*P:(ic+1)*P]),
                        f"f3_wl{buf}", 16)

        @blk.tensor
        def _(t):
            for hc in range(4):
                st = hc % 2
                if hc >= 1:
                    t.wait_ge(pr.sem("f3_y"), hc)
                for ic in range(ICD):
                    n = hc * ICD + ic
                    buf = n % 4
                    t.wait_ge(pr.sem(f"f3_wl{buf}"), 64 * (n // 4 + 1))
                    first, last = ic == 0, ic == ICD - 1
                    mm = None
                    for tb in range(TB):
                        xh = s_sh[:, buf, tb*P:(tb+1)*P]
                        xl = s_sl[:, buf, tb*P:(tb+1)*P]
                        t.matmul(ps[:, tb, :], xh, s_wwh[:, buf, :], start=first, stop=False)
                        t.matmul(ps[:, tb, :], xl, s_wwh[:, buf, :], start=False, stop=False)
                        mm = t.matmul(ps[:, tb, :], xh, s_wwl[:, buf, :], start=False, stop=last)
                    pr.inc(mm, "f3_ic")

        @blk.vector
        def _(v):
            for hc in range(4):
                st = hc % 2
                v.wait_ge(pr.sem("f3_ic"), (hc + 1) * ICD)
                if hc >= 2:
                    v.wait_ge(pr.sem("f3_yf"), hc - 1)
                op = None
                for tb in range(TB):
                    op = v.tensor_copy(out=s_y[:, st, tb, :], in_=ps[:, tb, :])
                pr.inc(op, "f3_y")

        @blk.gpsimd
        def _(g_):
            for hc in range(4):
                st = hc % 2
                g_.wait_ge(pr.sem("f3_y"), hc + 1)
                for tb in range(TB):
                    pr.inc(g_.dma_start(out=d_y1[tb*P:(tb+1)*P, hc*512:(hc+1)*512],
                                        in_=s_y[:, st, tb, :]), "f3_ys", 16)
                g_.wait_ge(pr.sem("f3_ys"), pr.val("f3_ys"))
                g_.sem_inc(pr.sem("f3_yf"), 1)
                pr.cnt["f3_yf"] += 1


def build(dump=(), phases=("a", "b", "c", "e", "f")):
    pr = Prog(dump)
    nc = pr.nc
    d_x = pr.inp("x", [B, 256, D], F32)
    i_ident = pr.inp("ident", [P, P], F32)

    d_xn = pr.dram("d_xn", [TOK, D], F32)
    d_xnt_h = pr.dram("d_xnt_h", [D, TOK], F16)
    d_xnt_l = pr.dram("d_xnt_l", [D, TOK], BF16)
    g_xnt_h = pr.dram("g_xnt_h", [8, D, TOK], F16)
    g_xnt_l = pr.dram("g_xnt_l", [8, D, TOK], BF16)
    pr.maybe_dump("xn", d_xn)
    pr.maybe_dump("g_xnt_h", g_xnt_h)
    pr.maybe_dump("g_xnt_l", g_xnt_l)

    with (
        nc.sbuf_tensor("c_ident", [P, P], F32) as s_ident,
    ):
        with nc.Block() as blk0:
            @blk0.sync
            def _(sync):
                pr.inc(sync.dma_start(out=s_ident[:], in_=i_ident[:, :]), "c_ld", 16)
                sync.wait_ge(pr.sem("c_ld"), pr.val("c_ld"))
        barrier(pr)
        recycle(pr)

        emit_ln_xnt(pr, d_x, d_xn, d_xnt_h, d_xnt_l, s_ident)
        barrier(pr)
        recycle(pr)

        with nc.Block() as blkg:
            @blkg.gpsimd
            def _(g):
                pr.inc(g.collective_compute("AllGather", Alu.bypass, RG,
                                            ins=[d_xnt_h[:, :]], outs=[g_xnt_h[:, :, :]]), "ag1")
                pr.inc(g.collective_compute("AllGather", Alu.bypass, RG,
                                            ins=[d_xnt_l[:, :]], outs=[g_xnt_l[:, :, :]]), "ag1")
                g.wait_ge(pr.sem("ag1"), 2)
        barrier(pr)
        recycle(pr)

        if "b" in phases:
            i_w = {pn: pr.inp(f"w{pn}", [256, D], F32) for pn in ("q", "k", "v")}
            i_w["o"] = pr.inp("wo", [D, D], F32)
            i_b = {pn: pr.inp(f"b{pn}", [2, P], F32) for pn in ("q", "k", "v")}
            wspl = {}
            for pn in ("q", "k", "v", "o"):
                R = 256 if pn != "o" else D
                wspl[pn] = (pr.dram(f"d_w{pn}_h", [R, D], F16),
                            pr.dram(f"d_w{pn}_l", [R, D], BF16))
            emit_wsplit(pr, [(i_w[pn], *wspl[pn]) for pn in ("q", "k", "v", "o")])
            barrier(pr)
            recycle(pr)
            dq = (pr.dram("d_q_h", [256, 8192], F16), pr.dram("d_q_l", [256, 8192], BF16))
            dk = (pr.dram("d_k_h", [256, 8192], F16), pr.dram("d_k_l", [256, 8192], BF16))
            dvn = (pr.dram("d_vn_h", [8192, 256], F16), pr.dram("d_vn_l", [8192, 256], BF16))
            for nm, pair in (("q", dq), ("k", dk), ("vn", dvn)):
                pr.maybe_dump(nm + "_h", pair[0])
                pr.maybe_dump(nm + "_l", pair[1])
            emit_qkv(pr, g_xnt_h, g_xnt_l, wspl, i_b, dq, dk, dvn, s_ident)
            barrier(pr)
            recycle(pr)

            i_masks = pr.inp("masks", [P, 4, 512], F32)
            d_at = pr.dram("d_at", [256, B, S], F32)
            pr.maybe_dump("at", d_at)
            emit_attn(pr, dq, dk, dvn, d_at, s_ident, i_masks)
            barrier(pr)
            recycle(pr)

            g_at = pr.dram("g_at", [8, 256, B, S], F32)
            with nc.Block() as blk3:
                @blk3.gpsimd
                def _(g):
                    pr.inc(g.collective_compute("AllGather", Alu.bypass, RG,
                                                ins=[d_at[:, :, :]], outs=[g_at[:, :, :, :]]),
                           "ag3")
                    g.wait_ge(pr.sem("ag3"), 1)
            barrier(pr)
            recycle(pr)

            i_bo = pr.inp("bo", [1, D], F32)
            i_ones = pr.inp("ones_row", [1, P], F32)
            d_o = pr.dram("d_o", [TOK, D], F32)
            pr.maybe_dump("o", d_o)
            emit_oproj(pr, g_at, wspl["o"], i_bo, i_ones, d_o)
            barrier(pr)
            recycle(pr)

        if "c" in phases:
            i_wg = pr.inp("wg_sh", [I0P, D], F32)
            i_wu = pr.inp("wu_sh", [I0P, D], F32)
            i_wd = pr.inp("wd_sh", [256, I0], F32)
            d_qwg_in = pr.dram("d_qwg_in", [I0P, D], BF16)
            d_qwu_in = pr.dram("d_qwu_in", [I0P, D], BF16)
            d_qwd_in = pr.dram("d_qwd_in", [256, I0], BF16)
            d_ws_in = pr.dram("d_ws_in", [1792], F32)
            emit_quant(pr, "cq1", i_wg, d_qwg_in, d_ws_in[0:768], I0P, D)
            emit_quant(pr, "cq2", i_wu, d_qwu_in, d_ws_in[768:1536], I0P, D)
            emit_quant(pr, "cq3", i_wd, d_qwd_in, d_ws_in[1536:1792], 256, I0)
            barrier(pr)
            recycle(pr)
            g_qwg = pr.dram("g_qwg", [8, I0P, D], BF16)
            g_qwu = pr.dram("g_qwu", [8, I0P, D], BF16)
            g_qwd = pr.dram("g_qwd", [8, 256, I0], BF16)
            g_ws = pr.dram("g_ws", [8, 1792], F32)
            d_wsrow = pr.dram("d_wsrow", [1, 14336], F32)
            with nc.Block() as blkc:
                @blkc.gpsimd
                def _(g):
                    for (i_t, o_t) in ((d_qwg_in, g_qwg), (d_qwu_in, g_qwu),
                                       (d_qwd_in, g_qwd), (d_ws_in, g_ws)):
                        idx = tuple(slice(None) for _ in i_t.shape)
                        odx = tuple(slice(None) for _ in o_t.shape)
                        pr.inc(g.collective_compute("AllGather", Alu.bypass, RG,
                                                    ins=[i_t[idx]], outs=[o_t[odx]]), "ag2")
                    g.wait_ge(pr.sem("ag2"), 4)
                    pr.inc(g.dma_start(out=d_wsrow[0, 0:6144].rearrange("(r a) -> r a", r=8),
                                       in_=g_ws[:, 0:768]), "ag2b", 16)
                    pr.inc(g.dma_start(out=d_wsrow[0, 6144:12288].rearrange("(r a) -> r a", r=8),
                                       in_=g_ws[:, 768:1536]), "ag2b", 16)
                    pr.inc(g.dma_start(out=d_wsrow[0, 12288:14336].rearrange("(r a) -> r a", r=8),
                                       in_=g_ws[:, 1536:1792]), "ag2b", 16)
                    g.wait_ge(pr.sem("ag2b"), 48)
            barrier(pr)
            recycle(pr)
            d_qx = pr.dram("d_qx", [TOK, D], BF16)
            d_sx = pr.dram("d_sx", [TOK], F32)
            pr.maybe_dump("qx", d_qx)
            pr.maybe_dump("sx", d_sx)
            pr.maybe_dump("qwg", g_qwg)
            pr.maybe_dump("wsrow", d_wsrow)
            emit_quant(pr, "c2", d_xn, d_qx, d_sx, TOK, D)
            barrier(pr)
            recycle(pr)
            d_t = pr.dram("d_t", [TOK, I0], F32)
            pr.maybe_dump("t", d_t)
            qwg_flat = g_qwg.rearrange("r a d -> (r a) d")
            qwu_flat = g_qwu.rearrange("r a d -> (r a) d")
            qwd_flat = g_qwd.rearrange("r a d -> (r a) d")
            emit_ffn0_gateup(pr, d_qx, d_sx, qwg_flat, qwu_flat, d_wsrow, d_t)
            barrier(pr)
            recycle(pr)
            d_qt = pr.dram("d_qt", [TOK, I0], BF16)
            d_st = pr.dram("d_st", [TOK], F32)
            emit_quant(pr, "c4", d_t, d_qt, d_st, TOK, I0)
            barrier(pr)
            recycle(pr)
            d_y0 = pr.dram("d_y0", [TOK, D], F32)
            pr.maybe_dump("y0", d_y0)
            emit_ffn0_down(pr, d_qt, d_st, qwd_flat, d_wsrow, d_y0)
            barrier(pr)
            recycle(pr)

            d_x1 = pr.dram("d_x1", [TOK, D], F32)
            pr.maybe_dump("x1", d_x1)
            emit_resid3(pr, "rd1", d_x, d_o, d_y0, d_x1)
            barrier(pr)
            recycle(pr)

        if "e" in phases:
            d_xn1 = pr.dram("d_xn1", [TOK, D], F32)
            d_xn1t_h = pr.dram("d_xn1t_h", [D, TOK], F16)
            d_xn1t_l = pr.dram("d_xn1t_l", [D, TOK], BF16)
            pr.maybe_dump("xn1", d_xn1)
            emit_ln_xnt(pr, d_x1, d_xn1, d_xn1t_h, d_xn1t_l, s_ident, tag="e1")
            barrier(pr)
            recycle(pr)

            e2_ins = {n: pr.inp(n, [P, DC, 8], F32) for n in ("arl", "aim", "cre", "cim")}
            e2_ins["ldt"] = pr.inp("ldt", [P, DC], F32)
            e2_ins["ident"] = i_ident
            d_k = pr.dram("d_k", [4, D], F32)
            pr.maybe_dump("k", d_k)
            emit_s4d_k(pr, e2_ins, d_k)
            barrier(pr)
            recycle(pr)

            i_dpar = pr.inp("dpar", [1, D], F32)
            d_mix = pr.dram("d_mix", [TOK, D], F32)
            pr.maybe_dump("mix", d_mix)
            emit_mix(pr, d_xn1, d_k, i_dpar, d_mix)
            barrier(pr)
            recycle(pr)

        if "f" in phases:
            RSH = I1 * D // 8
            taps = [[pr.inp(f"tap{mi}_{k}", [RSH], F32) for k in range(4)]
                    for mi in range(3)]
            bks = [pr.inp(f"bk{k}", [RSH], F32) for k in range(4)]
            w_sh = []
            w_gl = []
            for mi, nm in enumerate(("g", "u", "d")):
                dh = pr.dram(f"d_w1{nm}_h", [RSH], F16)
                dl = pr.dram(f"d_w1{nm}_l", [RSH], BF16)
                w_sh.append((dh, dl))
                w_gl.append((pr.dram(f"g_w1{nm}_h", [8, RSH], F16),
                             pr.dram(f"g_w1{nm}_l", [8, RSH], BF16)))
            emit_recon(pr, taps, bks, w_sh)
            barrier(pr)
            recycle(pr)
            with nc.Block() as blkf:
                @blkf.gpsimd
                def _(g):
                    for mi in range(3):
                        for j in range(2):
                            pr.inc(g.collective_compute(
                                "AllGather", Alu.bypass, RG,
                                ins=[w_sh[mi][j][:]], outs=[w_gl[mi][j][:, :]]), "ag4")
                    g.wait_ge(pr.sem("ag4"), 6)
            barrier(pr)
            recycle(pr)

            w1g = tuple(w_gl[0][j].rearrange("r a -> (r a)").rearrange("(i h) -> i h", h=D)
                        for j in range(2))
            w1u = tuple(w_gl[1][j].rearrange("r a -> (r a)").rearrange("(i h) -> i h", h=D)
                        for j in range(2))
            w1d = tuple(w_gl[2][j].rearrange("r a -> (r a)").rearrange("(h i) -> h i", i=I1)
                        for j in range(2))
            d_t1 = (pr.dram("d_t1_h", [TOK, I1], F16), pr.dram("d_t1_l", [TOK, I1], BF16))
            pr.maybe_dump("t1_h", d_t1[0])
            pr.maybe_dump("t1_l", d_t1[1])
            emit_ffn1_gu(pr, (d_xn1t_h, d_xn1t_l), w1g, w1u, d_t1)
            barrier(pr)
            recycle(pr)
            d_y1 = pr.dram("d_y1", [TOK, D], F32)
            pr.maybe_dump("y1", d_y1)
            emit_ffn1_down(pr, d_t1, w1d, d_y1)
            barrier(pr)
            recycle(pr)

            o_out = pr.out("out", [B, 256, D], F32)
            emit_resid3(pr, "rd2", d_x1, d_mix, d_y1, o_out)
            barrier(pr)

        pr.emit_dumps()
    return pr


def _declared_inputs(pr):
    import concourse.mybir as mb
    names = set()
    for alloc in pr.nc.m.functions[0].allocations:
        if isinstance(alloc, mb.MemoryLocationSet) and alloc.kind == "ExternalInput":
            names.add(alloc.memorylocations[0].name)
    return names


def run(pr, inputs, **kw):
    in_maps_full = host_in_maps(inputs)
    names = _declared_inputs(pr)
    in_maps = [{k: v for k, v in m.items() if k in names} for m in in_maps_full]
    return run_bass_kernel_spmd(pr.nc, in_maps, core_ids=list(range(8)), **kw)


def kernel(**inputs):
    pr = build()
    res = run(pr, inputs)
    outs = [r["out"] for r in res.results]
    return np.concatenate(outs, axis=1)


# version 03
# speedup vs baseline: 1.0565x; 1.0565x over previous
import numpy as np
import ml_dtypes
import concourse.bass as bass
import concourse.mybir as mybir
from concourse.bass_utils import run_bass_kernel_spmd

F32 = mybir.dt.float32
BF16 = mybir.dt.bfloat16
F16 = mybir.dt.float16
FP8 = mybir.dt.float8e4
U8 = mybir.dt.uint8
Act = mybir.ActivationFunctionType
Alu = mybir.AluOpType
AX = mybir.AxisListType
RG = [list(range(8))]

BF = ml_dtypes.bfloat16
F8 = ml_dtypes.float8_e4m3

P = 128
D = 2048
TOK = 1024
TB = TOK // P
DC = D // P
S = 2048
B = 4
NH = 16
HD = 128
I0 = 5632
I1 = 8192
I0P = 768
INV_SQRT_HD = float(1.0 / np.sqrt(np.float32(HD)))
NEG_BIG = float(np.float32(-1e30))

N_CTRL = 130156
_SPLINE_CACHE = {}

def spline_consts():
    if "v" not in _SPLINE_CACHE:
        n = I1 * D
        t = np.linspace(0.0, N_CTRL - 1.0, n, dtype=np.float64)
        i = np.clip(np.floor(t).astype(np.int64), 0, N_CTRL - 2)
        f = (t - i).astype(np.float32)
        idx = np.stack([np.clip(i + k, 0, N_CTRL - 1) for k in (-1, 0, 1, 2)], axis=0)
        f2, f3 = f * f, f * f * f
        bk = np.stack([
            0.5 * (-f + 2 * f2 - f3),
            0.5 * (2.0 - 5 * f2 + 3 * f3),
            0.5 * (f + 4 * f2 - 3 * f3),
            0.5 * (-f2 + f3),
        ]).astype(np.float32)
        _SPLINE_CACHE["v"] = (idx, bk)
    return _SPLINE_CACHE["v"]

def host_in_maps(inp):
    idx, bk = spline_consts()
    n = I1 * D
    R = n // 8
    invp = np.argsort(np.asarray(inp["L0_ffn_down_reorder_index"]))
    wd2 = np.ascontiguousarray(np.asarray(inp["L0_ffn_w_down"])[:, invp])
    wg = np.asarray(inp["L0_ffn_w_gate"])
    wu = np.asarray(inp["L0_ffn_w_up"])
    x = np.asarray(inp["L0_attn_x"])
    wq, wk, wv = (np.asarray(inp[f"L0_attn_w{t}"]) for t in "qkv")
    bq, bkk, bv = (np.asarray(inp[f"L0_attn_b{t}"]) for t in "qkv")
    wo = np.ascontiguousarray(np.asarray(inp["L0_attn_wo"]))
    bo = np.asarray(inp["L0_attn_bo"])
    cps = [np.asarray(inp["L1_ffn_gate_cp"]), np.asarray(inp["L1_ffn_up_cp"]),
           np.asarray(inp["L1_ffn_down_cp"])]

    ident = np.eye(P, dtype=np.float32)
    ones_row = np.ones((1, P), dtype=np.float32)
    masks = np.zeros((4, P, 512), dtype=np.float32)
    for off in range(4):
        qi = off * 128 + np.arange(P)[:, None]
        kj = np.arange(512)[None, :]
        masks[off] = np.where(qi >= kj, 0.0, NEG_BIG)
    masks = np.ascontiguousarray(masks.transpose(1, 0, 2))

    def dcn(a):
        return np.ascontiguousarray(np.asarray(a).reshape(DC, P, 8).transpose(1, 0, 2))
    l1_small = {
        "arl": dcn(inp["L1_attn_A_real_log"]), "aim": dcn(inp["L1_attn_A_imag"]),
        "cre": dcn(inp["L1_attn_C_re"]), "cim": dcn(inp["L1_attn_C_im"]),
        "ldt": np.ascontiguousarray(np.asarray(inp["L1_attn_log_dt"]).reshape(DC, P).T),
        "dpar": np.ascontiguousarray(np.asarray(inp["L1_attn_D_param"]).reshape(1, D)),
    }

    in_maps = []
    for c in range(8):
        m = {}
        m["x"] = np.ascontiguousarray(x[:, 256*c:256*(c+1), :])
        for nm, w, b2 in (("q", wq, bq), ("k", wk, bkk), ("v", wv, bv)):
            m[f"w{nm}"] = np.ascontiguousarray(w[256*c:256*(c+1), :])
            m[f"b{nm}"] = np.ascontiguousarray(b2[256*c:256*(c+1)].reshape(2, P))
        m["wo"] = wo
        m["bo"] = np.ascontiguousarray(bo.reshape(1, D))

        def padrows(w, r0, r1, tot):
            sl = w[r0:min(r1, tot), :]
            if sl.shape[0] < r1 - r0:
                sl = np.concatenate([sl, np.zeros((r1 - r0 - sl.shape[0], w.shape[1]), w.dtype)])
            return np.ascontiguousarray(sl)
        m["wg_sh"] = padrows(wg, I0P*c, I0P*(c+1), I0)
        m["wu_sh"] = padrows(wu, I0P*c, I0P*(c+1), I0)
        m["wd_sh"] = np.ascontiguousarray(wd2[256*c:256*(c+1), :])

        sl = slice(R*c, R*(c+1))
        for mi, cp in enumerate(cps):
            for k in range(4):
                m[f"tap{mi}_{k}"] = np.ascontiguousarray(cp[idx[k, sl]]).astype(np.float32)
        for k in range(4):
            m[f"bk{k}"] = np.ascontiguousarray(bk[k, sl])
        m.update(l1_small)
        m["ident"] = ident
        m["ones_row"] = ones_row
        m["masks"] = masks
        in_maps.append(m)
    return in_maps

class Prog:
    def __init__(self, dump=()):
        self.nc = bass.Bass("TRN2", target_bir_lowering=False, detect_race_conditions=False)
        self.dump = set(dump)
        self.dumps = []
        self.sems = {}
        self.cnt = {}
        self.dma_names = set()

    def sem(self, name):
        if name not in self.sems:
            self.sems[name] = self.nc.alloc_semaphore(name)
            self.cnt[name] = 0
        return self.sems[name]

    def inc(self, instr, name, k=1):
        instr.then_inc(self.sem(name), k)
        self.cnt[name] += k
        if k >= 16 or name.startswith("ag"):
            self.dma_names.add(name)
        return self.cnt[name]

    def val(self, name):
        self.sem(name)
        return self.cnt[name]

    def fence(self, eng, op, name="hz"):
        n = self.inc(op, name)
        eng.wait_ge(self.sem(name), n)
        return op

    def dram(self, name, shape, dt):
        return self.nc.dram_tensor(name, list(shape), dt, kind="Internal").ap()

    def inp(self, name, shape, dt):
        return self.nc.dram_tensor(name, list(shape), dt, kind="ExternalInput").ap()

    def out(self, name, shape, dt):
        return self.nc.dram_tensor(name, list(shape), dt, kind="ExternalOutput").ap()

    def maybe_dump(self, name, ap):
        if name in self.dump:
            self.dumps.append((name, ap))

    def emit_dumps(self):
        if not self.dumps:
            return
        nc = self.nc
        with nc.Block() as blk:
            @blk.sync
            def _(sync):
                for name, ap in self.dumps:
                    o = self.out("dump_" + name, ap.shape, ap.dtype)
                    idxs = tuple(slice(None) for _ in ap.shape)
                    self.inc(sync.dma_start(out=o[idxs], in_=ap[idxs]), "dumpst", 16)
                sync.wait_ge(self.sem("dumpst"), self.val("dumpst"))


def barrier(pr):
    pr.nc.all_engine_barrier()


def recycle(pr, keep=()):
    nc = pr.nc
    names = [n for n in list(pr.sems) if n not in keep and n not in pr.dma_names]
    if not names:
        return
    with nc.Block() as blk:
        @blk.gpsimd
        def _(g):
            for n in names:
                g.sem_clear(pr.sems[n])
    pr.nc.all_engine_barrier()
    for n in names:
        nc.release_semaphore(pr.sems[n])
        del pr.sems[n]
        del pr.cnt[n]


def emit_ln_xnt(pr, d_x, d_xn, d_xnt_h, d_xnt_l, s_ident, tag="a"):
    nc = pr.nc
    with (
        nc.sbuf_tensor(f"{tag}_x", [P, 2, D], F32) as s_x,
        nc.sbuf_tensor(f"{tag}_st", [P, 6], F32) as s_st,
        nc.sbuf_tensor(f"{tag}_xn", [P, 2, D], F32) as s_xn,
        nc.sbuf_tensor(f"{tag}_th", [P, 2, D], F16) as s_th,
        nc.sbuf_tensor(f"{tag}_tl", [P, 2, D], BF16) as s_tl,
        nc.psum_tensor(f"{tag}_ps", [P, DC * 128], F32) as ps_t,
        nc.Block() as blk,
    ):
        @blk.sync
        def _(sync):
            for tb in range(TB):
                b, sh = divmod(tb, 2)
                if tb >= 2:
                    sync.wait_ge(pr.sem(f"{tag}_fin"), tb - 1)
                xin = (d_x[b, sh*P:(sh+1)*P, :] if len(d_x.shape) == 3
                       else d_x[tb*P:(tb+1)*P, :])
                pr.inc(sync.dma_start(out=s_x[:, tb % 2, :], in_=xin),
                       f"{tag}_ldx{tb % 2}", 16)

        @blk.vector
        def _(v):
            def evac(te):
                if te < 0:
                    return
                if te >= 2:
                    v.wait_ge(pr.sem(f"{tag}_fin"), te - 1)
                v.wait_ge(pr.sem(f"{tag}_hi"), te + 1)
                for dc in range(DC):
                    op = v.tensor_tensor(out=s_tl[:, te % 2, dc*128:(dc+1)*128],
                                         in0=ps_t[:, dc*128:(dc+1)*128],
                                         in1=s_th[:, te % 2, dc*128:(dc+1)*128],
                                         op=Alu.subtract)
                    if dc == DC - 1:
                        pr.inc(op, f"{tag}_ev")
            for tb in range(TB):
                buf = tb % 2
                v.wait_ge(pr.sem(f"{tag}_ldx{buf}"), 16 * (tb // 2 + 1))
                if tb >= 2:
                    v.wait_ge(pr.sem(f"{tag}_tr"), tb - 1)
                    v.wait_ge(pr.sem(f"{tag}_fin"), tb - 1)
                xt = s_x[:, buf, :]
                F = lambda op: pr.fence(v, op, f"{tag}_hz")
                F(v.tensor_reduce(out=s_st[:, 0:1], in_=xt, axis=AX.X, op=Alu.add))
                v.tensor_tensor(out=s_xn[:, buf, :], in0=xt, in1=xt, op=Alu.mult)
                F(v.tensor_reduce(out=s_st[:, 1:2], in_=s_xn[:, buf, :], axis=AX.X, op=Alu.add))
                F(v.tensor_scalar(out=s_st[:, 0:1], in0=s_st[:, 0:1], scalar1=1.0/D,
                                  scalar2=None, op0=Alu.mult))
                F(v.tensor_scalar(out=s_st[:, 1:2], in0=s_st[:, 1:2], scalar1=1.0/D,
                                  scalar2=None, op0=Alu.mult))
                F(v.tensor_tensor(out=s_st[:, 2:3], in0=s_st[:, 0:1],
                                  in1=s_st[:, 0:1], op=Alu.mult))
                F(v.tensor_tensor(out=s_st[:, 1:2], in0=s_st[:, 1:2], in1=s_st[:, 2:3],
                                  op=Alu.subtract))
                pr.inc(v.tensor_scalar(out=s_st[:, 1:2], in0=s_st[:, 1:2], scalar1=1e-5,
                                       scalar2=None, op0=Alu.add), f"{tag}_var")
                v.wait_ge(pr.sem(f"{tag}_sqrt"), tb + 1)
                F(v.reciprocal(out=s_st[:, 4:5], in_=s_st[:, 3:4]))
                F(v.tensor_tensor(out=s_st[:, 5:6], in0=s_st[:, 1:2], in1=s_st[:, 4:5],
                                  op=Alu.mult))
                F(v.tensor_tensor(out=s_st[:, 2:3], in0=s_st[:, 3:4], in1=s_st[:, 5:6],
                                  op=Alu.add))
                F(v.tensor_scalar(out=s_st[:, 2:3], in0=s_st[:, 2:3],
                                  scalar1=0.5, scalar2=None, op0=Alu.mult))
                F(v.reciprocal(out=s_st[:, 4:5], in_=s_st[:, 2:3]))
                pr.inc(v.tensor_scalar(out=s_xn[:, buf, :], in0=xt,
                                       scalar1=s_st[:, 0:1], scalar2=s_st[:, 4:5],
                                       op0=Alu.subtract, op1=Alu.mult), f"{tag}_xn")
                evac(tb - 1)
            evac(TB - 1)

        @blk.scalar
        def _(sc):
            def his(te):
                if te < 0:
                    return
                if te >= 2:
                    sc.wait_ge(pr.sem(f"{tag}_fin"), te - 1)
                sc.wait_ge(pr.sem(f"{tag}_tr"), te + 1)
                for dc in range(DC):
                    op = sc.activation(out=s_th[:, te % 2, dc*128:(dc+1)*128],
                                       in_=ps_t[:, dc*128:(dc+1)*128],
                                       func=Act.Copy)
                    if dc == DC - 1:
                        pr.inc(op, f"{tag}_hi")
            for tb in range(TB):
                sc.wait_ge(pr.sem(f"{tag}_var"), tb + 1)
                pr.inc(sc.activation(out=s_st[:, 3:4], in_=s_st[:, 1:2], func=Act.Sqrt),
                       f"{tag}_sqrt")
                his(tb - 1)
            his(TB - 1)

        @blk.tensor
        def _(t):
            for tb in range(TB):
                buf = tb % 2
                t.wait_ge(pr.sem(f"{tag}_xn"), tb + 1)
                if tb >= 1:
                    t.wait_ge(pr.sem(f"{tag}_ev"), tb)
                for dc in range(DC):
                    op = t.transpose(ps_t[:, dc*128:(dc+1)*128],
                                     s_xn[:, buf, dc*128:(dc+1)*128], s_ident[:, :])
                    if dc == DC - 1:
                        pr.inc(op, f"{tag}_tr")

        @blk.gpsimd
        def _(g):
            for tb in range(TB):
                buf = tb % 2
                g.wait_ge(pr.sem(f"{tag}_xn"), tb + 1)
                pr.inc(g.dma_start(out=d_xn[tb*128:(tb+1)*128, :], in_=s_xn[:, buf, :]),
                       f"{tag}_stxn", 16)
                g.wait_ge(pr.sem(f"{tag}_ev"), tb + 1)
                pr.inc(g.dma_start(
                    out=d_xnt_h.rearrange("(c p) t -> p c t", p=P)[:, :, tb*128:(tb+1)*128],
                    in_=s_th[:, buf, :].rearrange("p (c q) -> p c q", q=128)), f"{tag}_stt", 16)
                pr.inc(g.dma_start(
                    out=d_xnt_l.rearrange("(c p) t -> p c t", p=P)[:, :, tb*128:(tb+1)*128],
                    in_=s_tl[:, buf, :].rearrange("p (c q) -> p c q", q=128)), f"{tag}_stt", 16)
                g.wait_ge(pr.sem(f"{tag}_stt"), pr.val(f"{tag}_stt"))
                g.wait_ge(pr.sem(f"{tag}_stxn"), pr.val(f"{tag}_stxn"))
                g.sem_inc(pr.sem(f"{tag}_fin"), 1)
                pr.cnt[f"{tag}_fin"] += 1



def emit_wsplit(pr, srcs, tag="b1"):
    nc = pr.nc
    C = srcs[0][0].shape[1]
    with (
        nc.sbuf_tensor(f"{tag}_w", [P, 2, C], F32) as s_w,
        nc.sbuf_tensor(f"{tag}_h", [P, 2, C], F16) as s_h,
        nc.sbuf_tensor(f"{tag}_l", [P, 2, C], BF16) as s_l,
        nc.Block() as blk,
    ):
        tiles = []
        for (src, dh, dl) in srcs:
            assert src.shape[1] == C
            for rb in range(src.shape[0] // P):
                tiles.append((src, dh, dl, rb))

        @blk.sync
        def _(sync):
            for i, (src, dh, dl, rb) in enumerate(tiles):
                if i >= 2:
                    sync.wait_ge(pr.sem(f"{tag}_fin"), i - 1)
                pr.inc(sync.dma_start(out=s_w[:, i % 2, :],
                                      in_=src[rb*P:(rb+1)*P, :]), f"{tag}_ld{i % 2}", 16)

        @blk.vector
        def _(v):
            for i, (src, dh, dl, rb) in enumerate(tiles):
                buf = i % 2
                v.wait_ge(pr.sem(f"{tag}_ld{buf}"), 16 * (i // 2 + 1))
                v.tensor_copy(out=s_h[:, buf, :], in_=s_w[:, buf, :])
                pr.inc(v.tensor_tensor(out=s_l[:, buf, :], in0=s_w[:, buf, :],
                                       in1=s_h[:, buf, :], op=Alu.subtract), f"{tag}_sp")

        @blk.gpsimd
        def _(g):
            for i, (src, dh, dl, rb) in enumerate(tiles):
                buf = i % 2
                g.wait_ge(pr.sem(f"{tag}_sp"), i + 1)
                pr.inc(g.dma_start(out=dh[rb*P:(rb+1)*P, :], in_=s_h[:, buf, :]), f"{tag}_st", 16)
                pr.inc(g.dma_start(out=dl[rb*P:(rb+1)*P, :], in_=s_l[:, buf, :]), f"{tag}_st", 16)
                g.wait_ge(pr.sem(f"{tag}_st"), pr.val(f"{tag}_st"))
                g.sem_inc(pr.sem(f"{tag}_fin"), 1)
                pr.cnt[f"{tag}_fin"] += 1


def emit_qkv(pr, g_xnt_h, g_xnt_l, wspl, bias_in, dq, dk, dvn, s_ident):
    nc = pr.nc
    TKC = 16
    PROJ = ("q", "k", "v")
    with (
        nc.sbuf_tensor("b2_wTh", [P, 3, 2, DC, P], F16) as s_wTh,
        nc.sbuf_tensor("b2_wTl", [P, 3, 2, DC, P], BF16) as s_wTl,
        nc.sbuf_tensor("b2_xh", [P, 2, DC, 512], F16) as s_xh,
        nc.sbuf_tensor("b2_xl", [P, 2, DC, 512], BF16) as s_xl,
        nc.sbuf_tensor("b2_b", [P, 3, 2], F32) as s_b,
        nc.sbuf_tensor("b2_qh", [P, 2, 512], F16) as s_qh,
        nc.sbuf_tensor("b2_ql", [P, 2, 512], BF16) as s_ql,
        nc.sbuf_tensor("b2_v32", [P, 2, 512], F32) as s_v32,
        nc.sbuf_tensor("b2_vh", [P, 2, 512], F16) as s_vh,
        nc.sbuf_tensor("b2_vl", [P, 2, 512], BF16) as s_vl,
        nc.psum_tensor("b2_ps", [P, 2, 512], F32) as ps,
        nc.psum_tensor("b2_pt", [P, 2, 512], F32) as ps_vt,
        nc.Block() as blk,
    ):
        def qk_m(tk, pi, ob):
            return tk * 4 + pi * 2 + ob

        @blk.scalar
        def _(sc):
            for pi, pn in enumerate(PROJ):
                dh, dl = wspl[pn]
                for ob in range(2):
                    for dc in range(DC):
                        pr.inc(sc.dma_start_transpose(
                            out=s_wTh[:, pi, ob, dc, :],
                            in_=dh[ob*P:(ob+1)*P, dc*P:(dc+1)*P]), "b2_wld", 16)
                        pr.inc(sc.dma_start_transpose(
                            out=s_wTl[:, pi, ob, dc, :],
                            in_=dl[ob*P:(ob+1)*P, dc*P:(dc+1)*P]), "b2_wld", 16)
            for pi, pn in enumerate(PROJ):
                for ob in range(2):
                    pr.inc(sc.dma_start(out=s_b[:, pi, ob:ob+1],
                                        in_=bias_in[pn][ob:ob+1, :].transpose([1, 0])),
                           "b2_wld", 16)
            sc.wait_ge(pr.sem("b2_wld"), pr.val("b2_wld"))
            sc.sem_inc(pr.sem("b2_wrdy"), 1)
            pr.cnt["b2_wrdy"] += 1
            for tk in range(TKC):
                for ob in range(2):
                    gv = tk * 2 + ob
                    sc.wait_ge(pr.sem("b2_vtr"), gv + 1)
                    if gv >= 2:
                        sc.wait_ge(pr.sem("b2_gv"), gv - 1)
                    pr.inc(sc.activation(out=s_vh[:, gv % 2, :], in_=ps_vt[:, gv % 2, :],
                                         func=Act.Copy), "b2_vhi")

        @blk.sync
        def _(sync):
            sync.wait_ge(pr.sem("ag1"), 2)
            for tk in range(TKC):
                r, half = divmod(tk, 2)
                if tk >= 2:
                    sync.wait_ge(pr.sem("b2_mm"), (tk - 1) * 6)
                pr.inc(sync.dma_start(
                    out=s_xh[:, tk % 2, :, :],
                    in_=g_xnt_h[r, :, half*512:(half+1)*512].rearrange("(c p) t -> p c t", p=P)),
                    f"b2_x{tk % 2}", 16)
                pr.inc(sync.dma_start(
                    out=s_xl[:, tk % 2, :, :],
                    in_=g_xnt_l[r, :, half*512:(half+1)*512].rearrange("(c p) t -> p c t", p=P)),
                    f"b2_x{tk % 2}", 16)

        @blk.tensor
        def _(t):
            t.wait_ge(pr.sem("b2_wld"), pr.val("b2_wld"))
            for tk in range(TKC):
                t.wait_ge(pr.sem(f"b2_x{tk % 2}"), 32 * (tk // 2 + 1))
                for pi, pn in enumerate(PROJ):
                    for ob in range(2):
                        g = tk * 6 + pi * 2 + ob
                        if g >= 2:
                            gp = g - 2
                            tkp, jp = divmod(gp, 6)
                            if jp < 4:
                                t.wait_ge(pr.sem("b2_elo"), tkp * 4 + jp + 1)
                            else:
                                t.wait_ge(pr.sem("b2_ef32"), tkp * 2 + (jp - 4) + 1)
                        mm = None
                        for dc in range(DC):
                            first, last = dc == 0, dc == DC - 1
                            wh = s_wTh[:, pi, ob, dc, :]
                            wl = s_wTl[:, pi, ob, dc, :]
                            xh = s_xh[:, tk % 2, dc, :]
                            xl = s_xl[:, tk % 2, dc, :]
                            t.matmul(ps[:, g % 2, :], wh, xh, start=first, stop=False)
                            t.matmul(ps[:, g % 2, :], wh, xl, start=False, stop=False)
                            mm = t.matmul(ps[:, g % 2, :], wl, xh, start=False, stop=last)
                        pr.inc(mm, "b2_mm")
                for ob in range(2):
                    gv = tk * 2 + ob
                    t.wait_ge(pr.sem("b2_ef32"), gv + 1)
                    if gv >= 2:
                        t.wait_ge(pr.sem("b2_vlo"), gv - 1)
                    mm = None
                    for q4 in range(4):
                        mm = t.transpose(ps_vt[:, gv % 2, q4*P:(q4+1)*P],
                                         s_v32[:, gv % 2, q4*P:(q4+1)*P], s_ident[:, :])
                    pr.inc(mm, "b2_vtr")

        @blk.vector
        def _(v):
            v.wait_ge(pr.sem("b2_wrdy"), 1)
            for tk in range(TKC):
                for pi, pn in enumerate(PROJ):
                    for ob in range(2):
                        g = tk * 6 + pi * 2 + ob
                        pb = g % 2
                        v.wait_ge(pr.sem("b2_mm"), g + 1)
                        if pn == "v":
                            gv = tk * 2 + ob
                            if gv >= 2:
                                v.wait_ge(pr.sem("b2_vtr"), gv - 1)
                            op = v.tensor_scalar(out=s_v32[:, gv % 2, :], in0=ps[:, pb, :],
                                                 scalar1=s_b[:, pi, ob:ob+1], scalar2=None,
                                                 op0=Alu.add)
                            pr.inc(op, "b2_ef32")
                            continue
                        m = qk_m(tk, pi, ob)
                        if m >= 2:
                            v.wait_ge(pr.sem("b2_gq"), m - 1)
                        v.tensor_scalar(out=s_qh[:, m % 2, :], in0=ps[:, pb, :],
                                        scalar1=s_b[:, pi, ob:ob+1], scalar2=None,
                                        op0=Alu.add)
                        op = v.scalar_tensor_tensor(out=s_ql[:, m % 2, :], in0=ps[:, pb, :],
                                                    scalar=s_b[:, pi, ob:ob+1],
                                                    in1=s_qh[:, m % 2, :],
                                                    op0=Alu.add, op1=Alu.subtract)
                        pr.inc(op, "b2_elo")
                for ob in range(2):
                    gv = tk * 2 + ob
                    v.wait_ge(pr.sem("b2_vhi"), gv + 1)
                    pr.inc(v.tensor_tensor(out=s_vl[:, gv % 2, :],
                                           in0=ps_vt[:, gv % 2, :],
                                           in1=s_vh[:, gv % 2, :],
                                           op=Alu.subtract), "b2_vlo")

        @blk.gpsimd
        def _(g_):
            for tk in range(TKC):
                for pi, pn in enumerate(PROJ[:2]):
                    for ob in range(2):
                        m = qk_m(tk, pi, ob)
                        dsth, dstl = (dq if pn == "q" else dk)
                        g_.wait_ge(pr.sem("b2_elo"), m + 1)
                        pr.inc(g_.dma_start(out=dsth[ob*P:(ob+1)*P, tk*512:(tk+1)*512],
                                            in_=s_qh[:, m % 2, :]), "b2_stq", 16)
                        pr.inc(g_.dma_start(out=dstl[ob*P:(ob+1)*P, tk*512:(tk+1)*512],
                                            in_=s_ql[:, m % 2, :]), "b2_stq", 16)
                        g_.wait_ge(pr.sem("b2_stq"), pr.val("b2_stq"))
                        g_.sem_inc(pr.sem("b2_gq"), 1)
                        pr.cnt["b2_gq"] += 1
                for ob in range(2):
                    gv = tk * 2 + ob
                    g_.wait_ge(pr.sem("b2_vlo"), gv + 1)
                    pr.inc(g_.dma_start(
                        out=dvn[0].rearrange("(tk q4 p) o -> p tk q4 o", p=P, q4=4)[:, tk, :, ob*P:(ob+1)*P],
                        in_=s_vh[:, gv % 2, :].rearrange("p (q4 o) -> p q4 o", o=P)), "b2_stv", 16)
                    pr.inc(g_.dma_start(
                        out=dvn[1].rearrange("(tk q4 p) o -> p tk q4 o", p=P, q4=4)[:, tk, :, ob*P:(ob+1)*P],
                        in_=s_vl[:, gv % 2, :].rearrange("p (q4 o) -> p q4 o", o=P)), "b2_stv", 16)
                    g_.wait_ge(pr.sem("b2_stv"), pr.val("b2_stv"))
                    g_.sem_inc(pr.sem("b2_gv"), 1)
                    pr.cnt["b2_gv"] += 1



def emit_attn(pr, dq, dk, dvn, d_at, s_ident, i_masks):
    nc = pr.nc
    C = INV_SQRT_HD
    with (
        nc.sbuf_tensor("b3_qh", [P, 2, 8, 256], F16) as s_Qh,
        nc.sbuf_tensor("b3_ql", [P, 2, 8, 256], BF16) as s_Ql,
        nc.sbuf_tensor("b3_kh", [P, 2, 8, 256], F16) as s_Kh,
        nc.sbuf_tensor("b3_kl", [P, 2, 8, 256], BF16) as s_Kl,
        nc.sbuf_tensor("b3_vh", [P, 2, 8, 2, P], F16) as s_Vh,
        nc.sbuf_tensor("b3_vl", [P, 2, 8, 2, P], BF16) as s_Vl,
        nc.sbuf_tensor("b3_mask", [P, 4, 512], F32) as s_mask,
        nc.sbuf_tensor("b3_p", [P, 2048], F32) as s_P,
        nc.sbuf_tensor("b3_pn", [P, 2048], F32) as s_Pn,
        nc.sbuf_tensor("b3_pth", [P, 16, 512], F16) as s_PTh,
        nc.sbuf_tensor("b3_ptl", [P, 16, 512], BF16) as s_PTl,
        nc.sbuf_tensor("b3_st", [P, 4], F32) as s_st,
        nc.sbuf_tensor("b3_at", [P, 2, 512], F32) as s_at,
        nc.psum_tensor("b3_psc", [P, 2048], F32) as ps_sc,
        nc.psum_tensor("b3_ppt", [P, 2, 512], F32) as ps_pt,
        nc.psum_tensor("b3_ppv", [P, 512], F32) as ps_pv,
        nc.Block() as blk,
    ):
        NR = 8
        def rqt(r, qt):
            return r * 16 + qt

        @blk.sync
        def _(sync):
            pr.inc(sync.dma_start(out=s_mask[:, :, :], in_=i_masks[:, :, :]), "b3_mld", 16)
            for r in range(NR):
                b, hl = divmod(r, 2)
                sl = r % 2
                if r >= 2:
                    sync.wait_ge(pr.sem("b3_pv"), (r - 1) * 4)
                rows = slice(hl * P, (hl + 1) * P)
                for (dst, src) in ((s_Qh, dq[0]), (s_Ql, dq[1]), (s_Kh, dk[0]), (s_Kl, dk[1])):
                    pr.inc(sync.dma_start(
                        out=dst[:, sl, :, :],
                        in_=src[rows, :].rearrange("o (rr t) -> o rr t", t=1024)[:, :, b*256:(b+1)*256]),
                        f"b3_ld{sl}", 16)
                for (dst, src) in ((s_Vh, dvn[0]), (s_Vl, dvn[1])):
                    for rr in range(8):
                        pr.inc(sync.dma_start(
                            out=dst[:, sl, rr, :, :],
                            in_=src.rearrange("(rr bb sh p) o -> p rr bb sh o", p=P, sh=2, bb=4)[:, rr, b, :, rows]),
                            f"b3_ld{sl}", 16)

        @blk.tensor
        def _(t):
            t.wait_ge(pr.sem("b3_mld"), 16)
            for r in range(NR):
                sl = r % 2
                t.wait_ge(pr.sem(f"b3_ld{sl}"), 320 * (r // 2 + 1))
                def scores(qt):
                    g = rqt(r, qt)
                    if g >= 1:
                        t.wait_ge(pr.sem("b3_exp"), g)
                    nkc = qt // 4 + 1
                    qsl = s_Qh[:, sl, qt // 2, (qt % 2)*128:(qt % 2)*128+128]
                    qsl_l = s_Ql[:, sl, qt // 2, (qt % 2)*128:(qt % 2)*128+128]
                    mm = None
                    for kc in range(nkc):
                        ksl_h = s_Kh[:, sl, kc*2:kc*2+2, :].rearrange("p a b -> p (a b)")
                        ksl_l = s_Kl[:, sl, kc*2:kc*2+2, :].rearrange("p a b -> p (a b)")
                        t.matmul(ps_sc[:, kc*512:(kc+1)*512], qsl, ksl_h,
                                 start=True, stop=False)
                        mm = t.matmul(ps_sc[:, kc*512:(kc+1)*512], qsl, ksl_l,
                                      start=False, stop=False)
                    for kc in range(nkc):
                        ksl_h = s_Kh[:, sl, kc*2:kc*2+2, :].rearrange("p a b -> p (a b)")
                        mm = t.matmul(ps_sc[:, kc*512:(kc+1)*512], qsl_l, ksl_h,
                                      start=False, stop=True)
                    pr.inc(mm, "b3_mm")
                def ptrans(qt):
                    g = rqt(r, qt)
                    t.wait_ge(pr.sem("b3_pn"), g + 1)
                    nq = qt // 4 + 1
                    for iq in range(nq):
                        gq = pr.cnt["b3_ptq"] = pr.cnt.get("b3_ptq", 0) + 1
                        if gq >= 3:
                            t.wait_ge(pr.sem("b3_ptlo"), gq - 2)
                        mm = None
                        for j in range(4):
                            kt = iq * 4 + j
                            mm = t.transpose(ps_pt[:, gq % 2, j*128:(j+1)*128],
                                             s_Pn[:, kt*128:(kt+1)*128], s_ident[:, :])
                        pr.inc(mm, "b3_pt")
                def pv(qc):
                    gq = r * 4 + qc
                    t.wait_ge(pr.sem("b3_ptlo"), r * 40 + 2 * (qc + 1) * (qc + 2))
                    if gq >= 1:
                        t.wait_ge(pr.sem("b3_aev"), gq)
                    nkt = qc * 4 + 4
                    mm = None
                    for kt in range(nkt):
                        vh = s_Vh[:, sl, kt // 2, kt % 2, :]
                        vl = s_Vl[:, sl, kt // 2, kt % 2, :]
                        first = kt == 0
                        t.matmul(ps_pv[:, :], vh, s_PTh[:, kt, :], start=first, stop=False)
                        t.matmul(ps_pv[:, :], vh, s_PTl[:, kt, :], start=False, stop=False)
                        mm = t.matmul(ps_pv[:, :], vl, s_PTh[:, kt, :], start=False,
                                      stop=(kt == nkt - 1))
                    pr.inc(mm, "b3_pv")
                for qt in range(16):
                    scores(qt)
                    if qt >= 1:
                        ptrans(qt - 1)
                    if qt >= 1 and (qt - 1) % 4 == 3:
                        pv((qt - 1) // 4)
                ptrans(15)
                pv(3)

        @blk.vector
        def _(v):
            v.wait_ge(pr.sem("b3_mld"), 16)
            for r in range(NR):
                sl = r % 2
                def ptlo(qt):
                    nq = qt // 4 + 1
                    for iq in range(nq):
                        gq = pr.cnt["b3_ptqv"] = pr.cnt.get("b3_ptqv", 0) + 1
                        v.wait_ge(pr.sem("b3_pth"), gq)
                        op = None
                        for j in range(4):
                            kt = iq * 4 + j
                            op = v.tensor_tensor(
                                out=s_PTl[:, kt, (qt % 4)*128:(qt % 4)*128+128],
                                in0=ps_pt[:, gq % 2, j*128:(j+1)*128],
                                in1=s_PTh[:, kt, (qt % 4)*128:(qt % 4)*128+128],
                                op=Alu.subtract)
                        pr.inc(op, "b3_ptlo")
                for qt in range(16):
                    g = rqt(r, qt)
                    nkc = qt // 4 + 1
                    W = nkc * 512
                    v.wait_ge(pr.sem("b3_mm"), g + 1)
                    v.tensor_tensor(out=ps_sc[:, (nkc-1)*512:nkc*512],
                                    in0=ps_sc[:, (nkc-1)*512:nkc*512],
                                    in1=s_mask[:, qt % 4, :], op=Alu.add)
                    F = lambda op: pr.fence(v, op, "b3_hz")
                    F(v.tensor_reduce(out=s_st[:, 0:1], in_=ps_sc[:, 0:W], axis=AX.X,
                                      op=Alu.max))
                    pr.inc(v.tensor_scalar(out=s_st[:, 1:2], in0=s_st[:, 0:1],
                                           scalar1=-C, scalar2=None, op0=Alu.mult),
                           "b3_bias")
                    if qt >= 1:
                        ptlo(qt - 1)
                    v.wait_ge(pr.sem("b3_exp"), g + 1)
                    F(v.reciprocal(out=s_st[:, 2:3], in_=s_st[:, 3:4]))
                    pr.inc(v.tensor_scalar(out=s_Pn[:, 0:W], in0=s_P[:, 0:W],
                                           scalar1=s_st[:, 2:3], scalar2=None,
                                           op0=Alu.mult), "b3_pn")
                ptlo(15)

        @blk.scalar
        def _(sc):
            for r in range(NR):
                def his(qt):
                    nq = qt // 4 + 1
                    for iq in range(nq):
                        gq = pr.cnt["b3_ptqs"] = pr.cnt.get("b3_ptqs", 0) + 1
                        sc.wait_ge(pr.sem("b3_pt"), gq)
                        op = None
                        for j in range(4):
                            kt = iq * 4 + j
                            op = sc.activation(
                                out=s_PTh[:, kt, (qt % 4)*128:(qt % 4)*128+128],
                                in_=ps_pt[:, gq % 2, j*128:(j+1)*128], func=Act.Copy)
                        pr.inc(op, "b3_pth")
                def aev(qc):
                    gq = r * 4 + qc
                    sc.wait_ge(pr.sem("b3_pv"), gq + 1)
                    if gq >= 2:
                        sc.wait_ge(pr.sem("b3_asf"), gq - 1)
                    pr.inc(sc.activation(out=s_at[:, gq % 2, :], in_=ps_pv[:, :],
                                         func=Act.Copy), "b3_aev")
                for qt in range(16):
                    g = rqt(r, qt)
                    nkc = qt // 4 + 1
                    W = nkc * 512
                    sc.wait_ge(pr.sem("b3_bias"), g + 1)
                    pr.inc(sc.activation(out=s_P[:, 0:W], in_=ps_sc[:, 0:W], func=Act.Exp,
                                         bias=s_st[:, 1:2], scale=C,
                                         accum_out=s_st[:, 3:4]), "b3_exp")
                    if qt >= 1:
                        his(qt - 1)
                        if (qt - 1) % 4 == 3:
                            aev((qt - 1) // 4)
                his(15)
                aev(3)

        @blk.gpsimd
        def _(g_):
            for r in range(NR):
                b, hl = divmod(r, 2)
                for qc in range(4):
                    gq = r * 4 + qc
                    g_.wait_ge(pr.sem("b3_aev"), gq + 1)
                    pr.inc(g_.dma_start(out=d_at[hl*P:(hl+1)*P, b, qc*512:(qc+1)*512],
                                        in_=s_at[:, gq % 2, :]), "b3_ast", 16)
                    g_.wait_ge(pr.sem("b3_ast"), pr.val("b3_ast"))
                    g_.sem_inc(pr.sem("b3_asf"), 1)
                    pr.cnt["b3_asf"] += 1



def emit_oproj(pr, g_at, wo_spl, i_bo, i_ones, d_o):
    nc = pr.nc
    with (
        nc.sbuf_tensor("b5_a32", [P, 2, TOK], F32) as s_a32,
        nc.sbuf_tensor("b5_ah", [P, DC, TOK], F16) as s_ah,
        nc.sbuf_tensor("b5_al", [P, DC, TOK], BF16) as s_al,
        nc.sbuf_tensor("b5_wh", [P, DC, 2, 512], F16) as s_wh,
        nc.sbuf_tensor("b5_wl", [P, DC, 2, 512], BF16) as s_wl,
        nc.sbuf_tensor("b5_bo", [P, D], F32) as s_bo,
        nc.sbuf_tensor("b5_bor", [P, D], F32) as s_bor,
        nc.sbuf_tensor("b5_on", [P, P], F32) as s_on,
        nc.sbuf_tensor("b5_o", [P, 2, 2, 512], F32) as s_o,
        nc.psum_tensor("b5_ps", [P, 4, 512], F32) as ps,
        nc.psum_tensor("b5_pb", [P, 512], F32) as ps_b,
        nc.Block() as blk,
    ):
        @blk.sync
        def _(sync):
            sync.wait_ge(pr.sem("ag3"), 1)
            pid = sync.partition_id()
            scol = bass.ts(pid, 256)
            pr.inc(sync.dma_start(out=s_bo[0:1, :], in_=i_bo[:, :]), "b5_bld", 16)
            pr.inc(sync.dma_start(out=s_on[0:1, :], in_=i_ones[:, :]), "b5_bld", 16)
            for dc in range(DC):
                rr, oh = divmod(dc, 2)
                if dc >= 2:
                    sync.wait_ge(pr.sem("b5_spl"), dc - 1)
                pr.inc(sync.dma_start(
                    out=s_a32[:, dc % 2, :],
                    in_=g_at[rr, oh*P:(oh+1)*P, :, scol]), f"b5_lda{dc % 2}", 16)

        @blk.vector
        def _(v):
            for dc in range(DC):
                v.wait_ge(pr.sem(f"b5_lda{dc % 2}"), 16 * (dc // 2 + 1))
                v.tensor_copy(out=s_ah[:, dc, :], in_=s_a32[:, dc % 2, :])
                pr.inc(v.tensor_tensor(out=s_al[:, dc, :], in0=s_a32[:, dc % 2, :],
                                       in1=s_ah[:, dc, :], op=Alu.subtract), "b5_spl")
            v.wait_ge(pr.sem("b5_bcp"), 4)
            for half in range(2):
                for tb in range(TB):
                    htb = half * TB + tb
                    v.wait_ge(pr.sem("b5_mm"), htb + 1)
                    if htb >= 2:
                        v.wait_ge(pr.sem("b5_sfin"), htb - 1)
                    op = None
                    for occ in range(2):
                        oc = half * 2 + occ
                        op = v.tensor_tensor(out=s_o[:, htb % 2, occ, :],
                                             in0=ps[:, (tb % 2)*2 + occ, :],
                                             in1=s_bor[:, oc*512:(oc+1)*512], op=Alu.add)
                    pr.inc(op, "b5_oev")

        @blk.scalar
        def _(sc):
            for oc4 in range(4):
                sc.wait_ge(pr.sem("b5_bmm"), oc4 + 1)
                pr.inc(sc.activation(out=s_bor[:, oc4*512:(oc4+1)*512], in_=ps_b[:, :],
                                     func=Act.Copy), "b5_bcp")
            for half in range(2):
                if half == 1:
                    sc.wait_ge(pr.sem("b5_mm"), 8)
                for dc in range(DC):
                    for occ in range(2):
                        oc = half * 2 + occ
                        pr.inc(sc.dma_start_transpose(
                            out=s_wh[:, dc, occ, :],
                            in_=wo_spl[0][oc*512:(oc+1)*512, dc*P:(dc+1)*P]), f"b5_wld{half}", 16)
                        pr.inc(sc.dma_start_transpose(
                            out=s_wl[:, dc, occ, :],
                            in_=wo_spl[1][oc*512:(oc+1)*512, dc*P:(dc+1)*P]), f"b5_wld{half}", 16)

        @blk.tensor
        def _(t):
            t.wait_ge(pr.sem("b5_bld"), 32)
            for oc4 in range(4):
                if oc4 >= 1:
                    t.wait_ge(pr.sem("b5_bcp"), oc4)
                pr.inc(t.matmul(ps_b[:, :], s_on[0:1, :], s_bo[0:1, oc4*512:(oc4+1)*512],
                                start=True, stop=True), "b5_bmm")
            for half in range(2):
                t.wait_ge(pr.sem(f"b5_wld{half}"), pr.val(f"b5_wld{half}"))
                for tb in range(TB):
                    htb = half * TB + tb
                    if htb >= 2:
                        t.wait_ge(pr.sem("b5_oev"), htb - 1)
                    for dc in range(DC):
                        if half == 0 and tb == 0:
                            t.wait_ge(pr.sem("b5_spl"), dc + 1)
                        first, last = dc == 0, dc == DC - 1
                        ah = s_ah[:, dc, tb*P:(tb+1)*P]
                        al = s_al[:, dc, tb*P:(tb+1)*P]
                        mm = None
                        for occ in range(2):
                            wh = s_wh[:, dc, occ, :]
                            wl = s_wl[:, dc, occ, :]
                            pb = (tb % 2) * 2 + occ
                            t.matmul(ps[:, pb, :], ah, wh, start=first, stop=False)
                            t.matmul(ps[:, pb, :], al, wh, start=False, stop=False)
                            mm = t.matmul(ps[:, pb, :], ah, wl, start=False, stop=last)
                        if last:
                            pr.inc(mm, "b5_mm")

        @blk.gpsimd
        def _(g_):
            for half in range(2):
                for tb in range(TB):
                    htb = half * TB + tb
                    g_.wait_ge(pr.sem("b5_oev"), htb + 1)
                    pr.inc(g_.dma_start(
                        out=d_o[tb*P:(tb+1)*P, half*1024:(half+1)*1024],
                        in_=s_o[:, htb % 2, :, :].rearrange("p a b -> p (a b)")), "b5_st", 16)
                    g_.wait_ge(pr.sem("b5_st"), pr.val("b5_st"))
                    g_.sem_inc(pr.sem("b5_sfin"), 1)
                    pr.cnt["b5_sfin"] += 1



def emit_quant(pr, tag, src, dst, dsc, R, C):
    nc = pr.nc
    NT = R // P
    with (
        nc.sbuf_tensor(f"{tag}_x", [P, 2, C], F32) as s_x,
        nc.sbuf_tensor(f"{tag}_v", [P, C], F32) as s_v,
        nc.sbuf_tensor(f"{tag}_sq", [P, C], F32) as s_sq,
        nc.sbuf_tensor(f"{tag}_m", [P, C], U8) as s_m,
        nc.sbuf_tensor(f"{tag}_d8", [P, C], FP8) as s_d8,
        nc.sbuf_tensor(f"{tag}_h8", [P, C], FP8) as s_h8,
        nc.sbuf_tensor(f"{tag}_q2", [P, C], BF16) as s_q2,
        nc.sbuf_tensor(f"{tag}_qb", [P, 2, C], BF16) as s_qb,
        nc.sbuf_tensor(f"{tag}_s", [P, 2, 4], F32) as s_st,
        nc.Block() as blk,
    ):
        @blk.sync
        def _(sync):
            for i in range(NT):
                if i >= 2:
                    sync.wait_ge(pr.sem(f"{tag}_fin"), i - 1)
                pr.inc(sync.dma_start(out=s_x[:, i % 2, :], in_=src[i*P:(i+1)*P, :]),
                       f"{tag}_ld{i % 2}", 16)

        @blk.vector
        def _(v):
            for i in range(NT):
                buf = i % 2
                F = lambda op: pr.fence(v, op, f"{tag}_hz")
                v.wait_ge(pr.sem(f"{tag}_ld{buf}"), 16 * (i // 2 + 1))
                if i >= 2:
                    v.wait_ge(pr.sem(f"{tag}_fin"), i - 1)
                xt = s_x[:, buf, :]
                F(v.tensor_reduce(out=s_st[:, buf, 0:1], in_=xt, axis=AX.X, op=Alu.max,
                                  apply_absolute_value=True))
                F(v.tensor_scalar(out=s_st[:, buf, 1:2], in0=s_st[:, buf, 0:1],
                                  scalar1=1e-30, scalar2=None, op0=Alu.max))
                F(v.reciprocal(out=s_st[:, buf, 2:3], in_=s_st[:, buf, 1:2]))
                F(v.tensor_scalar(out=s_st[:, buf, 2:3], in0=s_st[:, buf, 2:3],
                                  scalar1=448.0, scalar2=None, op0=Alu.mult))
                pr.inc(v.tensor_scalar(out=s_st[:, buf, 3:4], in0=s_st[:, buf, 0:1],
                                       scalar1=1.0/448.0, scalar2=None, op0=Alu.mult),
                       f"{tag}_sc")
                pr.inc(v.tensor_scalar(out=s_v[:, :], in0=xt, scalar1=s_st[:, buf, 2:3],
                                       scalar2=None, op0=Alu.mult), f"{tag}_v")
                v.tensor_tensor(out=s_sq[:, :], in0=s_v[:, :], in1=s_v[:, :], op=Alu.mult)
                v.tensor_scalar(out=s_m[:, :], in0=s_sq[:, :], scalar1=2.0**-8,
                                scalar2=None, op0=Alu.is_ge)
                v.tensor_scalar(out=s_d8[:, :], in0=s_v[:, :], scalar1=-0.0625,
                                scalar2=0.0625, op0=Alu.max, op1=Alu.min)
                v.tensor_scalar(out=s_qb[:, buf, :], in0=s_d8[:, :], scalar1=1.0,
                                scalar2=None, op0=Alu.mult)
                v.wait_ge(pr.sem(f"{tag}_h8"), i + 1)
                v.tensor_scalar(out=s_q2[:, :], in0=s_h8[:, :], scalar1=2.0,
                                scalar2=None, op0=Alu.mult)
                pr.inc(v.copy_predicated(out=s_qb[:, buf, :], mask=s_m[:, :],
                                         data=s_q2[:, :]), f"{tag}_q")

        @blk.scalar
        def _(sc):
            for i in range(NT):
                sc.wait_ge(pr.sem(f"{tag}_v"), i + 1)
                pr.inc(sc.activation(out=s_h8[:, :], in_=s_v[:, :], func=Act.Copy,
                                     scale=0.5), f"{tag}_h8")

        @blk.gpsimd
        def _(g):
            for i in range(NT):
                buf = i % 2
                g.wait_ge(pr.sem(f"{tag}_q"), i + 1)
                g.wait_ge(pr.sem(f"{tag}_sc"), i + 1)
                pr.inc(g.dma_start(out=dst[i*P:(i+1)*P, :], in_=s_qb[:, buf, :]),
                       f"{tag}_st", 16)
                pr.inc(g.dma_start(out=dsc[i*P:(i+1)*P], in_=s_st[:, buf, 3:4]),
                       f"{tag}_st", 16)
                g.wait_ge(pr.sem(f"{tag}_st"), pr.val(f"{tag}_st"))
                g.sem_inc(pr.sem(f"{tag}_fin"), 1)
                pr.cnt[f"{tag}_fin"] += 1


def emit_ffn0_gateup(pr, d_qx, d_sx, qwg_flat, qwu_flat, d_wsrow, d_t):
    nc = pr.nc
    IC0 = I0 // 512
    with (
        nc.sbuf_tensor("c3_qxT", [P, DC, TB, P], BF16) as s_qxT,
        nc.sbuf_tensor("c3_w", [P, 2, DC, 2, 512], BF16) as s_w,
        nc.sbuf_tensor("c3_wsc", [P, 2, 2, 512], F32) as s_wsc,
        nc.sbuf_tensor("c3_sx", [P, TB], F32) as s_sx,
        nc.sbuf_tensor("c3_sg", [P, 2, 512], F32) as s_sg,
        nc.sbuf_tensor("c3_sg2", [P, 2, 512], F32) as s_sg2,
        nc.sbuf_tensor("c3_t", [P, 2, 2, 512], F32) as s_t,
        nc.psum_tensor("c3_ps", [P, 8, 512], F32) as ps,
        nc.Block() as blk,
    ):
        NST = IC0 * 4

        @blk.sync
        def _(sync):
            pr.inc(sync.dma_start(out=s_sx[:, :],
                                  in_=d_sx.rearrange("(t p) -> p t", p=P),
                                  allow_slow_non_contiguous=True), "c3_sxl", 16)
            for dc in range(DC):
                for tb in range(TB):
                    pr.inc(sync.dma_start_transpose(
                        out=s_qxT[:, dc, tb, :],
                        in_=d_qx[tb*P:(tb+1)*P, dc*P:(dc+1)*P]), "c3_pl", 16)
            sync.wait_ge(pr.sem("ag2b"), 48)
            for ic in range(IC0):
                if ic >= 2:
                    sync.wait_ge(pr.sem("c3_t"), (ic - 1) * 4)
                pr.inc(sync.dma_start(
                    out=s_wsc[:, ic % 2, 0, :],
                    in_=d_wsrow[:, ic*512:(ic+1)*512].broadcast_to([P, 512])),
                    f"c3_wl{ic % 2}", 16)
                pr.inc(sync.dma_start(
                    out=s_wsc[:, ic % 2, 1, :],
                    in_=d_wsrow[:, 6144+ic*512:6144+(ic+1)*512].broadcast_to([P, 512])),
                    f"c3_wl{ic % 2}", 16)
                for dc in range(DC):
                    pr.inc(sync.dma_start_transpose(
                        out=s_w[:, ic % 2, dc, 0, :],
                        in_=qwg_flat[ic*512:(ic+1)*512, dc*P:(dc+1)*P]), f"c3_wl{ic % 2}", 16)
                    pr.inc(sync.dma_start_transpose(
                        out=s_w[:, ic % 2, dc, 1, :],
                        in_=qwu_flat[ic*512:(ic+1)*512, dc*P:(dc+1)*P]), f"c3_wl{ic % 2}", 16)

        @blk.tensor
        def _(t):
            t.wait_ge(pr.sem("c3_pl"), 16 * (DC * TB))
            for ic in range(IC0):
                t.wait_ge(pr.sem(f"c3_wl{ic % 2}"), 16 * 34 * (ic // 2 + 1))
                for tp in range(4):
                    g = ic * 4 + tp
                    st = g % 2
                    if g >= 2:
                        t.wait_ge(pr.sem("c3_t"), g - 1)
                    mm = None
                    for dc in range(DC):
                        first, last = dc == 0, dc == DC - 1
                        for mi in range(2):
                            for tq in range(2):
                                tb = tp * 2 + tq
                                mm = t.matmul(ps[:, st*4 + mi*2 + tq, :],
                                              s_qxT[:, dc, tb, :],
                                              s_w[:, ic % 2, dc, mi, :],
                                              start=first, stop=last)
                    pr.inc(mm, "c3_mm")

        @blk.vector
        def _(v):
            v.wait_ge(pr.sem("c3_sxl"), 16)
            for ic in range(IC0):
                for tp in range(4):
                    g = ic * 4 + tp
                    st = g % 2
                    v.wait_ge(pr.sem("c3_mm"), g + 1)
                    if g >= 2:
                        v.wait_ge(pr.sem("c3_tf"), g - 1)
                    for tq in range(2):
                        pr.inc(v.tensor_tensor(out=s_sg[:, tq, :], in0=ps[:, st*4 + tq, :],
                                               in1=s_wsc[:, ic % 2, 0, :], op=Alu.mult),
                               "c3_eg")
                    for tq in range(2):
                        tb = tp * 2 + tq
                        v.wait_ge(pr.sem("c3_sg"), g * 2 + tq + 1)
                        v.tensor_tensor(out=s_t[:, st, tq, :], in0=ps[:, st*4 + 2 + tq, :],
                                        in1=s_wsc[:, ic % 2, 1, :], op=Alu.mult)
                        v.tensor_scalar(out=s_t[:, st, tq, :], in0=s_t[:, st, tq, :],
                                        scalar1=s_sx[:, tb:tb+1], scalar2=None,
                                        op0=Alu.mult)
                        op = v.tensor_tensor(out=s_t[:, st, tq, :], in0=s_t[:, st, tq, :],
                                             in1=s_sg2[:, tq, :], op=Alu.mult)
                        if tq == 1:
                            pr.inc(op, "c3_t")

        @blk.scalar
        def _(sc):
            sc.wait_ge(pr.sem("c3_sxl"), 16)
            for ic in range(IC0):
                for tp in range(4):
                    g = ic * 4 + tp
                    st = g % 2
                    for tq in range(2):
                        tb = tp * 2 + tq
                        sc.wait_ge(pr.sem("c3_eg"), g * 2 + tq + 1)
                        pr.inc(sc.activation(out=s_sg2[:, tq, :], in_=s_sg[:, tq, :],
                                             func=Act.Silu, scale=s_sx[:, tb:tb+1]),
                               "c3_sg")

        @blk.gpsimd
        def _(g_):
            for ic in range(IC0):
                for tp in range(4):
                    g = ic * 4 + tp
                    st = g % 2
                    g_.wait_ge(pr.sem("c3_t"), g + 1)
                    for tq in range(2):
                        tb = tp * 2 + tq
                        pr.inc(g_.dma_start(out=d_t[tb*P:(tb+1)*P, ic*512:(ic+1)*512],
                                            in_=s_t[:, st, tq, :]), "c3_tst", 16)
                    g_.wait_ge(pr.sem("c3_tst"), pr.val("c3_tst"))
                    g_.sem_inc(pr.sem("c3_tf"), 1)
                    pr.cnt["c3_tf"] += 1


def emit_ffn0_down(pr, d_qt, d_st, qwd_flat, d_wsrow, d_y0):
    nc = pr.nc
    ICD = I0 // P
    with (
        nc.sbuf_tensor("c5_qtT", [P, ICD, TB, P], BF16) as s_qtT,
        nc.sbuf_tensor("c5_w", [P, 8, 512], BF16) as s_w,
        nc.sbuf_tensor("c5_wsc", [P, 2, 512], F32) as s_wsc,
        nc.sbuf_tensor("c5_st", [P, TB], F32) as s_stc,
        nc.sbuf_tensor("c5_y", [P, 2, 4, 512], F32) as s_y,
        nc.psum_tensor("c5_ps", [P, 8, 512], F32) as ps,
        nc.Block() as blk,
    ):
        @blk.sync
        def _(sync):
            pr.inc(sync.dma_start(out=s_stc[:, :],
                                  in_=d_st.rearrange("(t p) -> p t", p=P),
                                  allow_slow_non_contiguous=True), "c5_stl", 16)
            for ic in range(ICD):
                for tb in range(TB):
                    pr.inc(sync.dma_start_transpose(
                        out=s_qtT[:, ic, tb, :],
                        in_=d_qt[tb*P:(tb+1)*P, ic*P:(ic+1)*P]), "c5_pl", 16)
            for hc in range(4):
                if hc >= 2:
                    sync.wait_ge(pr.sem("c5_y"), 2 * hc - 2)
                pr.inc(sync.dma_start(
                    out=s_wsc[:, hc % 2, :],
                    in_=d_wsrow[:, 12288+hc*512:12288+(hc+1)*512].broadcast_to([P, 512])),
                    f"c5_ws{hc % 2}", 16)
                for half in range(2):
                    g = hc * 2 + half
                    for ic in range(ICD):
                        n = g * ICD + ic
                        buf = n % 8
                        if n >= 8:
                            sync.wait_ge(pr.sem("c5_ic"), n - 7)
                        pr.inc(sync.dma_start_transpose(
                            out=s_w[:, buf, :],
                            in_=qwd_flat[hc*512:(hc+1)*512, ic*P:(ic+1)*P]),
                            f"c5_wl{buf}", 16)

        @blk.tensor
        def _(t):
            t.wait_ge(pr.sem("c5_pl"), 16 * (ICD * TB))
            for hc in range(4):
                for half in range(2):
                    g = hc * 2 + half
                    st = g % 2
                    if g >= 2:
                        t.wait_ge(pr.sem("c5_y"), g - 1)
                    for ic in range(ICD):
                        n = g * ICD + ic
                        buf = n % 8
                        t.wait_ge(pr.sem(f"c5_wl{buf}"), 16 * (n // 8 + 1))
                        first, last = ic == 0, ic == ICD - 1
                        mm = None
                        for tq in range(4):
                            tb = half * 4 + tq
                            mm = t.matmul(ps[:, st*4 + tq, :], s_qtT[:, ic, tb, :],
                                          s_w[:, buf, :], start=first, stop=last)
                        pr.inc(mm, "c5_ic")

        @blk.vector
        def _(v):
            v.wait_ge(pr.sem("c5_stl"), 16)
            for hc in range(4):
                for half in range(2):
                    g = hc * 2 + half
                    st = g % 2
                    v.wait_ge(pr.sem("c5_ic"), (g + 1) * ICD)
                    v.wait_ge(pr.sem(f"c5_ws{hc % 2}"), 16 * (hc // 2 + 1))
                    if g >= 2:
                        v.wait_ge(pr.sem("c5_yf"), g - 1)
                    op = None
                    for tq in range(4):
                        tb = half * 4 + tq
                        v.tensor_tensor(out=s_y[:, st, tq, :], in0=ps[:, st*4 + tq, :],
                                        in1=s_wsc[:, hc % 2, :], op=Alu.mult)
                        op = v.tensor_scalar(out=s_y[:, st, tq, :], in0=s_y[:, st, tq, :],
                                             scalar1=s_stc[:, tb:tb+1], scalar2=None,
                                             op0=Alu.mult)
                    pr.inc(op, "c5_y")

        @blk.gpsimd
        def _(g_):
            for hc in range(4):
                for half in range(2):
                    g = hc * 2 + half
                    st = g % 2
                    g_.wait_ge(pr.sem("c5_y"), g + 1)
                    for tq in range(4):
                        tb = half * 4 + tq
                        pr.inc(g_.dma_start(out=d_y0[tb*P:(tb+1)*P, hc*512:(hc+1)*512],
                                            in_=s_y[:, st, tq, :]), "c5_yst", 16)
                    g_.wait_ge(pr.sem("c5_yst"), pr.val("c5_yst"))
                    g_.sem_inc(pr.sem("c5_yf"), 1)
                    pr.cnt["c5_yf"] += 1



def emit_resid3(pr, tag, src_x, src_a, src_b, dst):
    nc = pr.nc
    with (
        nc.sbuf_tensor(f"{tag}_x", [P, 2, D], F32) as s_x,
        nc.sbuf_tensor(f"{tag}_a", [P, 2, D], F32) as s_a,
        nc.sbuf_tensor(f"{tag}_b", [P, 2, D], F32) as s_b,
        nc.sbuf_tensor(f"{tag}_o", [P, 2, D], F32) as s_o,
        nc.Block() as blk,
    ):
        @blk.sync
        def _(sync):
            for tb in range(TB):
                if tb >= 2:
                    sync.wait_ge(pr.sem(f"{tag}_fin"), tb - 1)
                if len(src_x.shape) == 3:
                    bb, sh = divmod(tb, 2)
                    xin = src_x[bb, sh*P:(sh+1)*P, :]
                else:
                    xin = src_x[tb*P:(tb+1)*P, :]
                pr.inc(sync.dma_start(out=s_x[:, tb % 2, :], in_=xin), f"{tag}_l{tb % 2}", 16)
                pr.inc(sync.dma_start(out=s_a[:, tb % 2, :],
                                      in_=src_a[tb*P:(tb+1)*P, :]), f"{tag}_l{tb % 2}", 16)
                pr.inc(sync.dma_start(out=s_b[:, tb % 2, :],
                                      in_=src_b[tb*P:(tb+1)*P, :]), f"{tag}_l{tb % 2}", 16)

        @blk.vector
        def _(v):
            for tb in range(TB):
                buf = tb % 2
                v.wait_ge(pr.sem(f"{tag}_l{buf}"), 48 * (tb // 2 + 1))
                if tb >= 2:
                    v.wait_ge(pr.sem(f"{tag}_fin"), tb - 1)
                v.tensor_tensor(out=s_o[:, buf, :], in0=s_x[:, buf, :],
                                in1=s_a[:, buf, :], op=Alu.add)
                pr.inc(v.tensor_tensor(out=s_o[:, buf, :], in0=s_o[:, buf, :],
                                       in1=s_b[:, buf, :], op=Alu.add), f"{tag}_s")

        @blk.gpsimd
        def _(g):
            for tb in range(TB):
                g.wait_ge(pr.sem(f"{tag}_s"), tb + 1)
                if len(dst.shape) == 3:
                    bb, sh = divmod(tb, 2)
                    dsl = dst[bb, sh*P:(sh+1)*P, :]
                else:
                    dsl = dst[tb*P:(tb+1)*P, :]
                pr.inc(g.dma_start(out=dsl, in_=s_o[:, tb % 2, :]),
                       f"{tag}_st", 16)
                g.wait_ge(pr.sem(f"{tag}_st"), pr.val(f"{tag}_st"))
                g.sem_inc(pr.sem(f"{tag}_fin"), 1)
                pr.cnt[f"{tag}_fin"] += 1



def emit_s4d_k(pr, ins, d_k):
    nc = pr.nc
    SH3, SH2 = [P, DC, 8], [P, DC]
    with (
        nc.sbuf_tensor("e2_in", [P, DC, 8 * 4], F32) as s_in,
        nc.sbuf_tensor("e2_ldt", [P, DC], F32) as s_ldt,
        nc.sbuf_tensor("e2_w", [P, DC, 8 * 12], F32) as s_w,
        nc.sbuf_tensor("e2_k", [P, DC, 4], F32) as s_k,
        nc.sbuf_tensor("e2_kt", [DC, P], F32) as s_kt,
        nc.sbuf_tensor("e2_ident", [P, P], F32) as s_id2,
        nc.sbuf_tensor("e2_hpi", [P, 1], F32) as s_hpi,
        nc.psum_tensor("e2_ps", [P, 128], F32) as ps,
        nc.Block() as blk,
    ):
        L = {n: s_in[:, :, i*8:(i+1)*8] for i, n in enumerate(("arl", "aim", "cre", "cim"))}
        W = lambda i: s_w[:, :, i*8:(i+1)*8]

        @blk.sync
        def _(sync):
            for i, n in enumerate(("arl", "aim", "cre", "cim")):
                pr.inc(sync.dma_start(out=L[n], in_=ins[n][:, :, :]), "e2_ld", 16)
            pr.inc(sync.dma_start(out=s_ldt[:, :], in_=ins["ldt"][:, :]), "e2_ld", 16)
            pr.inc(sync.dma_start(out=s_id2[:, :], in_=ins["ident"][:, :]), "e2_ld", 16)

        @blk.scalar
        def _(sc):
            sc.wait_ge(pr.sem("e2_ld"), 96)
            sc_f = pr.fence(sc, sc.memzero(s_hpi[:, :]), "e2_shz") if False else None
            pr.inc(sc.activation(out=s_ldt[:, :], in_=s_ldt[:, :], func=Act.Exp), "e2_dt")
            sc.wait_ge(pr.sem("e2_dt8"), 1)
            pr.inc(sc.activation(out=W(1), in_=L["arl"], func=Act.Exp), "e2_are")
            sc.wait_ge(pr.sem("e2_dta"), 1)
            sc.wait_ge(pr.sem("e2_hpi"), 1)
            sc.activation(out=W(4), in_=W(2), func=Act.Exp)
            sc.activation(out=W(5), in_=W(3), func=Act.Sin, scale=-1.0,
                          bias=s_hpi[:, 0:1])
            pr.inc(sc.activation(out=W(6), in_=W(3), func=Act.Sin), "e2_trig")

        @blk.vector
        def _(v):
            F = lambda op: pr.fence(v, op, "e2_hz")
            pr.inc(v.memset(s_hpi[:, :], float(np.pi / 2)), "e2_hpi")
            v.wait_ge(pr.sem("e2_dt"), 1)
            for n in range(8):
                op = v.tensor_copy(out=s_w[:, :, 0*8+n:0*8+n+1], in_=s_ldt[:, :].unsqueeze(2))
            F(op)
            pr.inc(v.engine_nop(), "e2_dt8")
            v.wait_ge(pr.sem("e2_are"), 1)
            F(v.tensor_scalar(out=W(1), in0=W(1), scalar1=-1.0, scalar2=None, op0=Alu.mult))
            v.tensor_tensor(out=W(2), in0=W(1), in1=W(0), op=Alu.mult)
            F(v.tensor_tensor(out=W(3), in0=L["aim"], in1=W(0), op=Alu.mult))
            pr.inc(v.engine_nop(), "e2_dta")
            v.wait_ge(pr.sem("e2_trig"), 1)
            v.tensor_tensor(out=W(7), in0=W(4), in1=W(5), op=Alu.mult)
            F(v.tensor_tensor(out=W(8), in0=W(4), in1=W(6), op=Alu.mult))
            F(v.tensor_scalar(out=W(4), in0=W(7), scalar1=-1.0, scalar2=None, op0=Alu.add))
            v.tensor_tensor(out=W(5), in0=W(1), in1=W(1), op=Alu.mult)
            v.tensor_tensor(out=W(6), in0=L["aim"], in1=L["aim"], op=Alu.mult)
            F(v.tensor_tensor(out=W(5), in0=W(5), in1=W(6), op=Alu.add))
            F(v.reciprocal(out=W(5), in_=W(5)))
            v.tensor_tensor(out=W(9), in0=L["cre"], in1=W(4), op=Alu.mult)
            F(v.tensor_tensor(out=W(6), in0=L["cim"], in1=W(8), op=Alu.mult))
            F(v.tensor_tensor(out=W(9), in0=W(9), in1=W(6), op=Alu.subtract))
            v.tensor_tensor(out=W(10), in0=L["cre"], in1=W(8), op=Alu.mult)
            F(v.tensor_tensor(out=W(6), in0=L["cim"], in1=W(4), op=Alu.mult))
            F(v.tensor_tensor(out=W(10), in0=W(10), in1=W(6), op=Alu.add))
            v.tensor_tensor(out=W(11), in0=W(9), in1=W(1), op=Alu.mult)
            F(v.tensor_tensor(out=W(6), in0=W(10), in1=L["aim"], op=Alu.mult))
            F(v.tensor_tensor(out=W(11), in0=W(11), in1=W(6), op=Alu.add))
            F(v.tensor_tensor(out=W(11), in0=W(11), in1=W(5), op=Alu.mult))
            v.tensor_tensor(out=W(6), in0=W(10), in1=W(1), op=Alu.mult)
            F(v.tensor_tensor(out=W(10), in0=W(9), in1=L["aim"], op=Alu.mult))
            F(v.tensor_tensor(out=W(6), in0=W(6), in1=W(10), op=Alu.subtract))
            F(v.tensor_tensor(out=W(6), in0=W(6), in1=W(5), op=Alu.mult))
            F(v.tensor_reduce(out=s_k[:, :, 0:1], in_=W(11), axis=AX.X, op=Alu.add))
            F(v.tensor_copy(out=W(0), in_=W(7)))
            F(v.tensor_copy(out=W(1), in_=W(8)))
            for mm_ in range(1, 4):
                v.tensor_tensor(out=W(9), in0=W(11), in1=W(7), op=Alu.mult)
                F(v.tensor_tensor(out=W(10), in0=W(6), in1=W(8), op=Alu.mult))
                F(v.tensor_tensor(out=W(9), in0=W(9), in1=W(10), op=Alu.subtract))
                F(v.tensor_reduce(out=s_k[:, :, mm_:mm_+1], in_=W(9), axis=AX.X, op=Alu.add))
                if mm_ < 3:
                    v.tensor_tensor(out=W(9), in0=W(7), in1=W(0), op=Alu.mult)
                    F(v.tensor_tensor(out=W(10), in0=W(8), in1=W(1), op=Alu.mult))
                    v.tensor_tensor(out=W(2), in0=W(7), in1=W(1), op=Alu.mult)
                    F(v.tensor_tensor(out=W(3), in0=W(8), in1=W(0), op=Alu.mult))
                    F(v.tensor_tensor(out=W(7), in0=W(9), in1=W(10), op=Alu.subtract))
                    F(v.tensor_tensor(out=W(8), in0=W(2), in1=W(3), op=Alu.add))
            pr.inc(v.tensor_scalar(out=s_k[:, :, :], in0=s_k[:, :, :], scalar1=2.0,
                                   scalar2=None, op0=Alu.mult), "e2_k")
            for mm_ in range(4):
                v.wait_ge(pr.sem("e2_ktr"), mm_ + 1)
                if mm_ >= 1:
                    v.wait_ge(pr.sem("e2_kst"), mm_)
                pr.inc(v.tensor_copy(out=s_kt[:, :], in_=ps[0:DC, 0:P]), "e2_kev")

        @blk.tensor
        def _(t):
            t.wait_ge(pr.sem("e2_k"), 1)
            for mm_ in range(4):
                if mm_ >= 1:
                    t.wait_ge(pr.sem("e2_kst"), mm_)
                pr.inc(t.transpose(ps[0:DC, 0:P], s_k[:, :, mm_:mm_+1].squeeze(2), s_id2[:, :]),
                       "e2_ktr")

        @blk.gpsimd
        def _(g):
            for mm_ in range(4):
                g.wait_ge(pr.sem("e2_kev"), mm_ + 1)
                pr.inc(g.dma_start(out=d_k[mm_, :].rearrange("(c p) -> c p", p=P),
                                   in_=s_kt[:, :]), "e2_kstd", 16)
                g.wait_ge(pr.sem("e2_kstd"), pr.val("e2_kstd"))
                g.sem_inc(pr.sem("e2_kst"), 1)
                pr.cnt["e2_kst"] += 1


def emit_mix(pr, d_xn1, d_k, i_dpar, d_mix):
    nc = pr.nc
    with (
        nc.sbuf_tensor("e3_kbc", [P, 4, D], F32) as s_kbc,
        nc.sbuf_tensor("e3_db", [P, D], F32) as s_db,
        nc.sbuf_tensor("e3_x", [P, 2, 4, D], F32) as s_x,
        nc.sbuf_tensor("e3_o", [P, 2, 4, D], F32) as s_o,
        nc.sbuf_tensor("e3_tmp", [P, D], F32) as s_tmp,
        nc.Block() as blk,
    ):
        @blk.sync
        def _(sync):
            for mm_ in range(4):
                pr.inc(sync.dma_start(out=s_kbc[:, mm_, :],
                                      in_=d_k[mm_:mm_+1, :].broadcast_to([P, D])),
                       "e3_kld", 16)
            pr.inc(sync.dma_start(out=s_db[:, :],
                                  in_=i_dpar[:, :].broadcast_to([P, D])), "e3_kld", 16)
            for sh in range(2):
                if sh >= 1:
                    pass
                for b in range(4):
                    pr.inc(sync.dma_start(out=s_x[:, sh, b, :],
                                          in_=d_xn1[(b*2 + sh)*P:(b*2 + sh + 1)*P, :]),
                           f"e3_x{sh}", 16)

        @blk.vector
        def _(v):
            v.wait_ge(pr.sem("e3_kld"), 80)
            f = pr.fence(v, v.tensor_tensor(out=s_kbc[:, 0, :], in0=s_kbc[:, 0, :],
                                            in1=s_db[:, :], op=Alu.add), "e3_hz")
            for sh in range(2):
                v.wait_ge(pr.sem(f"e3_x{sh}"), 64)
                for b in range(4):
                    v.tensor_tensor(out=s_o[:, sh, b, :], in0=s_kbc[:, 0, :],
                                    in1=s_x[:, sh, b, :], op=Alu.mult)
                    for mm_ in range(1, b + 1):
                        v.tensor_tensor(out=s_tmp[:, :], in0=s_kbc[:, mm_, :],
                                        in1=s_x[:, sh, b - mm_, :], op=Alu.mult)
                        v.tensor_tensor(out=s_o[:, sh, b, :], in0=s_o[:, sh, b, :],
                                        in1=s_tmp[:, :], op=Alu.add)
                pr.inc(v.engine_nop(), "e3_mx")

        @blk.gpsimd
        def _(g):
            for sh in range(2):
                g.wait_ge(pr.sem("e3_mx"), sh + 1)
                for b in range(4):
                    pr.inc(g.dma_start(out=d_mix[(b*2 + sh)*P:(b*2 + sh + 1)*P, :],
                                       in_=s_o[:, sh, b, :]), "e3_st", 16)
            g.wait_ge(pr.sem("e3_st"), pr.val("e3_st"))



def emit_recon(pr, taps, bks, outs):
    nc = pr.nc
    RSH = I1 * D // 8
    NT = RSH // 65536
    with (
        nc.sbuf_tensor("f1_tp", [P, 2, 4, 512], F32) as s_tp,
        nc.sbuf_tensor("f1_bk", [P, 2, 4, 512], F32) as s_bk,
        nc.sbuf_tensor("f1_w", [P, 2, 512], F32) as s_w,
        nc.sbuf_tensor("f1_a", [P, 512], F32) as s_a,
        nc.sbuf_tensor("f1_h", [P, 2, 512], F16) as s_h,
        nc.sbuf_tensor("f1_l", [P, 2, 512], BF16) as s_l,
        nc.Block() as blk,
    ):
        jobs = [(mi, ti) for mi in range(3) for ti in range(NT)]

        @blk.sync
        def _(sync):
            for j, (mi, ti) in enumerate(jobs):
                buf = j % 2
                if j >= 2:
                    sync.wait_ge(pr.sem("f1_fin"), j - 1)
                sl = slice(ti * 65536, (ti + 1) * 65536)
                for k in range(4):
                    pr.inc(sync.dma_start(
                        out=s_tp[:, buf, k, :],
                        in_=taps[mi][k][sl].rearrange("(p f) -> p f", f=512)),
                        f"f1_ld{buf}", 16)
                    pr.inc(sync.dma_start(
                        out=s_bk[:, buf, k, :],
                        in_=bks[k][sl].rearrange("(p f) -> p f", f=512)),
                        f"f1_ld{buf}", 16)

        @blk.vector
        def _(v):
            for j, (mi, ti) in enumerate(jobs):
                buf = j % 2
                v.wait_ge(pr.sem(f"f1_ld{buf}"), 128 * (j // 2 + 1))
                if j >= 2:
                    v.wait_ge(pr.sem("f1_fin"), j - 1)
                v.tensor_tensor(out=s_w[:, buf, :], in0=s_bk[:, buf, 0, :],
                                in1=s_tp[:, buf, 0, :], op=Alu.mult)
                for k in range(1, 4):
                    v.tensor_tensor(out=s_a[:, :], in0=s_bk[:, buf, k, :],
                                    in1=s_tp[:, buf, k, :], op=Alu.mult)
                    v.tensor_tensor(out=s_w[:, buf, :], in0=s_w[:, buf, :],
                                    in1=s_a[:, :], op=Alu.add)
                v.tensor_copy(out=s_h[:, buf, :], in_=s_w[:, buf, :])
                pr.inc(v.tensor_tensor(out=s_l[:, buf, :], in0=s_w[:, buf, :],
                                       in1=s_h[:, buf, :], op=Alu.subtract), "f1_sp")

        @blk.gpsimd
        def _(g):
            for j, (mi, ti) in enumerate(jobs):
                buf = j % 2
                g.wait_ge(pr.sem("f1_sp"), j + 1)
                sl = slice(ti * 65536, (ti + 1) * 65536)
                pr.inc(g.dma_start(out=outs[mi][0][sl].rearrange("(p f) -> p f", f=512),
                                   in_=s_h[:, buf, :]), "f1_st", 16)
                pr.inc(g.dma_start(out=outs[mi][1][sl].rearrange("(p f) -> p f", f=512),
                                   in_=s_l[:, buf, :]), "f1_st", 16)
                g.wait_ge(pr.sem("f1_st"), pr.val("f1_st"))
                g.sem_inc(pr.sem("f1_fin"), 1)
                pr.cnt["f1_fin"] += 1


def emit_ffn1_gu(pr, d_xn1t, w1g, w1u, d_t1):
    nc = pr.nc
    IC1 = I1 // 512
    with (
        nc.sbuf_tensor("f2_xh", [P, DC, TB, P], F16) as s_xh,
        nc.sbuf_tensor("f2_xl", [P, DC, TB, P], BF16) as s_xl,
        nc.sbuf_tensor("f2_wh", [P, 2, DC, 2, 512], F16) as s_wh,
        nc.sbuf_tensor("f2_wl", [P, 2, DC, 2, 512], BF16) as s_wl,
        nc.sbuf_tensor("f2_sg", [P, 2, 512], F32) as s_sg2,
        nc.sbuf_tensor("f2_t", [P, 512], F32) as s_t,
        nc.sbuf_tensor("f2_th", [P, 2, 2, 512], F16) as s_th,
        nc.sbuf_tensor("f2_tl", [P, 2, 2, 512], BF16) as s_tl,
        nc.psum_tensor("f2_ps", [P, 8, 512], F32) as ps,
        nc.Block() as blk,
    ):
        @blk.sync
        def _(sync):
            for dc in range(DC):
                for tb in range(TB):
                    pr.inc(sync.dma_start(
                        out=s_xh[:, dc, tb, :],
                        in_=d_xn1t[0][dc*P:(dc+1)*P, tb*P:(tb+1)*P]), "f2_pl", 16)
                    pr.inc(sync.dma_start(
                        out=s_xl[:, dc, tb, :],
                        in_=d_xn1t[1][dc*P:(dc+1)*P, tb*P:(tb+1)*P]), "f2_pl", 16)
            sync.wait_ge(pr.sem("ag4"), 6)
            for ic in range(IC1):
                if ic >= 2:
                    sync.wait_ge(pr.sem("f2_t"), (ic - 1) * 4)
                for dc in range(DC):
                    for mi, wsp in enumerate((w1g, w1u)):
                        pr.inc(sync.dma_start_transpose(
                            out=s_wh[:, ic % 2, dc, mi, :],
                            in_=wsp[0][ic*512:(ic+1)*512, dc*P:(dc+1)*P]),
                            f"f2_wl{ic % 2}", 16)
                        pr.inc(sync.dma_start_transpose(
                            out=s_wl[:, ic % 2, dc, mi, :],
                            in_=wsp[1][ic*512:(ic+1)*512, dc*P:(dc+1)*P]),
                            f"f2_wl{ic % 2}", 16)

        @blk.tensor
        def _(t):
            t.wait_ge(pr.sem("f2_pl"), 16 * DC * TB * 2)
            for ic in range(IC1):
                t.wait_ge(pr.sem(f"f2_wl{ic % 2}"), 16 * 64 * (ic // 2 + 1))
                for tp in range(4):
                    g = ic * 4 + tp
                    st = g % 2
                    if g >= 2:
                        t.wait_ge(pr.sem("f2_t"), g - 1)
                    mm = None
                    for dc in range(DC):
                        first, last = dc == 0, dc == DC - 1
                        for mi in range(2):
                            for tq in range(2):
                                tb = tp * 2 + tq
                                pb = st * 4 + mi * 2 + tq
                                xh = s_xh[:, dc, tb, :]
                                xl = s_xl[:, dc, tb, :]
                                wh = s_wh[:, ic % 2, dc, mi, :]
                                wl = s_wl[:, ic % 2, dc, mi, :]
                                t.matmul(ps[:, pb, :], xh, wh, start=first, stop=False)
                                t.matmul(ps[:, pb, :], xl, wh, start=False, stop=False)
                                mm = t.matmul(ps[:, pb, :], xh, wl, start=False, stop=last)
                    pr.inc(mm, "f2_mm")

        @blk.scalar
        def _(sc):
            for ic in range(IC1):
                for tp in range(4):
                    g = ic * 4 + tp
                    st = g % 2
                    sc.wait_ge(pr.sem("f2_mm"), g + 1)
                    if g >= 1:
                        sc.wait_ge(pr.sem("f2_t"), g)
                    op = None
                    for tq in range(2):
                        op = sc.activation(out=s_sg2[:, tq, :], in_=ps[:, st*4 + tq, :],
                                           func=Act.Silu)
                    pr.inc(op, "f2_sg")

        @blk.vector
        def _(v):
            for ic in range(IC1):
                for tp in range(4):
                    g = ic * 4 + tp
                    st = g % 2
                    v.wait_ge(pr.sem("f2_sg"), g + 1)
                    if g >= 2:
                        v.wait_ge(pr.sem("f2_tf"), g - 1)
                    op = None
                    for tq in range(2):
                        v.tensor_tensor(out=s_t[:, :], in0=ps[:, st*4 + 2 + tq, :],
                                        in1=s_sg2[:, tq, :], op=Alu.mult)
                        v.tensor_copy(out=s_th[:, st, tq, :], in_=s_t[:, :])
                        op = v.tensor_tensor(out=s_tl[:, st, tq, :], in0=s_t[:, :],
                                             in1=s_th[:, st, tq, :], op=Alu.subtract)
                    pr.inc(op, "f2_t")

        @blk.gpsimd
        def _(g_):
            for ic in range(IC1):
                for tp in range(4):
                    g = ic * 4 + tp
                    st = g % 2
                    g_.wait_ge(pr.sem("f2_t"), g + 1)
                    for tq in range(2):
                        tb = tp * 2 + tq
                        pr.inc(g_.dma_start(out=d_t1[0][tb*P:(tb+1)*P, ic*512:(ic+1)*512],
                                            in_=s_th[:, st, tq, :]), "f2_ts", 16)
                        pr.inc(g_.dma_start(out=d_t1[1][tb*P:(tb+1)*P, ic*512:(ic+1)*512],
                                            in_=s_tl[:, st, tq, :]), "f2_ts", 16)
                    g_.wait_ge(pr.sem("f2_ts"), pr.val("f2_ts"))
                    g_.sem_inc(pr.sem("f2_tf"), 1)
                    pr.cnt["f2_tf"] += 1


def emit_ffn1_down(pr, d_t1, w1d, d_y1):
    nc = pr.nc
    ICD = I1 // P
    with (
        nc.sbuf_tensor("f3_sh", [P, 4, TOK], F16) as s_sh,
        nc.sbuf_tensor("f3_sl", [P, 4, TOK], BF16) as s_sl,
        nc.sbuf_tensor("f3_wh", [P, 4, 512], F16) as s_wwh,
        nc.sbuf_tensor("f3_wl", [P, 4, 512], BF16) as s_wwl,
        nc.sbuf_tensor("f3_y", [P, 2, 8, 512], F32) as s_y,
        nc.psum_tensor("f3_ps", [P, 8, 512], F32) as ps,
        nc.Block() as blk,
    ):
        @blk.sync
        def _(sync):
            for hc in range(4):
                for ic in range(ICD):
                    n = hc * ICD + ic
                    buf = n % 4
                    if n >= 4:
                        sync.wait_ge(pr.sem("f3_ic"), n - 3)
                    pr.inc(sync.dma_start_transpose(
                        out=s_sh[:, buf, :], in_=d_t1[0][:, ic*P:(ic+1)*P]),
                        f"f3_wl{buf}", 16)
                    pr.inc(sync.dma_start_transpose(
                        out=s_sl[:, buf, :], in_=d_t1[1][:, ic*P:(ic+1)*P]),
                        f"f3_wl{buf}", 16)
                    pr.inc(sync.dma_start_transpose(
                        out=s_wwh[:, buf, :], in_=w1d[0][hc*512:(hc+1)*512, ic*P:(ic+1)*P]),
                        f"f3_wl{buf}", 16)
                    pr.inc(sync.dma_start_transpose(
                        out=s_wwl[:, buf, :], in_=w1d[1][hc*512:(hc+1)*512, ic*P:(ic+1)*P]),
                        f"f3_wl{buf}", 16)

        @blk.tensor
        def _(t):
            for hc in range(4):
                st = hc % 2
                if hc >= 1:
                    t.wait_ge(pr.sem("f3_y"), hc)
                for ic in range(ICD):
                    n = hc * ICD + ic
                    buf = n % 4
                    t.wait_ge(pr.sem(f"f3_wl{buf}"), 64 * (n // 4 + 1))
                    first, last = ic == 0, ic == ICD - 1
                    mm = None
                    for tb in range(TB):
                        xh = s_sh[:, buf, tb*P:(tb+1)*P]
                        xl = s_sl[:, buf, tb*P:(tb+1)*P]
                        t.matmul(ps[:, tb, :], xh, s_wwh[:, buf, :], start=first, stop=False)
                        t.matmul(ps[:, tb, :], xl, s_wwh[:, buf, :], start=False, stop=False)
                        mm = t.matmul(ps[:, tb, :], xh, s_wwl[:, buf, :], start=False, stop=last)
                    pr.inc(mm, "f3_ic")

        @blk.vector
        def _(v):
            for hc in range(4):
                st = hc % 2
                v.wait_ge(pr.sem("f3_ic"), (hc + 1) * ICD)
                if hc >= 2:
                    v.wait_ge(pr.sem("f3_yf"), hc - 1)
                op = None
                for tb in range(TB):
                    op = v.tensor_copy(out=s_y[:, st, tb, :], in_=ps[:, tb, :])
                pr.inc(op, "f3_y")

        @blk.gpsimd
        def _(g_):
            for hc in range(4):
                st = hc % 2
                g_.wait_ge(pr.sem("f3_y"), hc + 1)
                for tb in range(TB):
                    pr.inc(g_.dma_start(out=d_y1[tb*P:(tb+1)*P, hc*512:(hc+1)*512],
                                        in_=s_y[:, st, tb, :]), "f3_ys", 16)
                g_.wait_ge(pr.sem("f3_ys"), pr.val("f3_ys"))
                g_.sem_inc(pr.sem("f3_yf"), 1)
                pr.cnt["f3_yf"] += 1


def build(dump=(), phases=("a", "b", "c", "e", "f")):
    pr = Prog(dump)
    nc = pr.nc
    d_x = pr.inp("x", [B, 256, D], F32)
    i_ident = pr.inp("ident", [P, P], F32)

    d_xn = pr.dram("d_xn", [TOK, D], F32)
    d_xnt_h = pr.dram("d_xnt_h", [D, TOK], F16)
    d_xnt_l = pr.dram("d_xnt_l", [D, TOK], BF16)
    g_xnt_h = pr.dram("g_xnt_h", [8, D, TOK], F16)
    g_xnt_l = pr.dram("g_xnt_l", [8, D, TOK], BF16)
    pr.maybe_dump("xn", d_xn)
    pr.maybe_dump("g_xnt_h", g_xnt_h)
    pr.maybe_dump("g_xnt_l", g_xnt_l)

    with (
        nc.sbuf_tensor("c_ident", [P, P], F32) as s_ident,
    ):
        with nc.Block() as blk0:
            @blk0.sync
            def _(sync):
                pr.inc(sync.dma_start(out=s_ident[:], in_=i_ident[:, :]), "c_ld", 16)
                sync.wait_ge(pr.sem("c_ld"), pr.val("c_ld"))
        barrier(pr)

        emit_ln_xnt(pr, d_x, d_xn, d_xnt_h, d_xnt_l, s_ident)
        barrier(pr)
        recycle(pr)

        with nc.Block() as blkg:
            @blkg.gpsimd
            def _(g):
                pr.inc(g.collective_compute("AllGather", Alu.bypass, RG,
                                            ins=[d_xnt_h[:, :]], outs=[g_xnt_h[:, :, :]]), "ag1")
                pr.inc(g.collective_compute("AllGather", Alu.bypass, RG,
                                            ins=[d_xnt_l[:, :]], outs=[g_xnt_l[:, :, :]]), "ag1")

        i_w = {pn: pr.inp(f"w{pn}", [256, D], F32) for pn in ("q", "k", "v")}
        i_w["o"] = pr.inp("wo", [D, D], F32)
        i_b = {pn: pr.inp(f"b{pn}", [2, P], F32) for pn in ("q", "k", "v")}
        wspl = {}
        for pn in ("q", "k", "v", "o"):
            R = 256 if pn != "o" else D
            wspl[pn] = (pr.dram(f"d_w{pn}_h", [R, D], F16),
                        pr.dram(f"d_w{pn}_l", [R, D], BF16))
        emit_wsplit(pr, [(i_w[pn], *wspl[pn]) for pn in ("q", "k", "v", "o")])
        barrier(pr)
        recycle(pr, keep=("ag1",))

        i_wg = pr.inp("wg_sh", [I0P, D], F32)
        i_wu = pr.inp("wu_sh", [I0P, D], F32)
        i_wd = pr.inp("wd_sh", [256, I0], F32)
        d_qwg_in = pr.dram("d_qwg_in", [I0P, D], BF16)
        d_qwu_in = pr.dram("d_qwu_in", [I0P, D], BF16)
        d_qwd_in = pr.dram("d_qwd_in", [256, I0], BF16)
        d_ws_in = pr.dram("d_ws_in", [1792], F32)
        emit_quant(pr, "cq1", i_wg, d_qwg_in, d_ws_in[0:768], I0P, D)
        barrier(pr)
        recycle(pr, keep=("ag1",))
        emit_quant(pr, "cq2", i_wu, d_qwu_in, d_ws_in[768:1536], I0P, D)
        barrier(pr)
        recycle(pr, keep=("ag1",))
        emit_quant(pr, "cq3", i_wd, d_qwd_in, d_ws_in[1536:1792], 256, I0)
        barrier(pr)
        recycle(pr, keep=("ag1",))
        g_qwg = pr.dram("g_qwg", [8, I0P, D], BF16)
        g_qwu = pr.dram("g_qwu", [8, I0P, D], BF16)
        g_qwd = pr.dram("g_qwd", [8, 256, I0], BF16)
        g_ws = pr.dram("g_ws", [8, 1792], F32)
        d_wsrow = pr.dram("d_wsrow", [1, 14336], F32)
        with nc.Block() as blkc:
            @blkc.gpsimd
            def _(g):
                for (i_t, o_t) in ((d_qwg_in, g_qwg), (d_qwu_in, g_qwu),
                                   (d_qwd_in, g_qwd), (d_ws_in, g_ws)):
                    idx = tuple(slice(None) for _ in i_t.shape)
                    odx = tuple(slice(None) for _ in o_t.shape)
                    pr.inc(g.collective_compute("AllGather", Alu.bypass, RG,
                                                ins=[i_t[idx]], outs=[o_t[odx]]), "ag2")
                g.wait_ge(pr.sem("ag2"), 4)
                pr.inc(g.dma_start(out=d_wsrow[0, 0:6144].rearrange("(r a) -> r a", r=8),
                                   in_=g_ws[:, 0:768]), "ag2b", 16)
                pr.inc(g.dma_start(out=d_wsrow[0, 6144:12288].rearrange("(r a) -> r a", r=8),
                                   in_=g_ws[:, 768:1536]), "ag2b", 16)
                pr.inc(g.dma_start(out=d_wsrow[0, 12288:14336].rearrange("(r a) -> r a", r=8),
                                   in_=g_ws[:, 1536:1792]), "ag2b", 16)

        RSH = I1 * D // 8
        taps = [[pr.inp(f"tap{mi}_{k}", [RSH], F32) for k in range(4)]
                for mi in range(3)]
        bks = [pr.inp(f"bk{k}", [RSH], F32) for k in range(4)]
        w_sh = []
        w_gl = []
        for mi, nm in enumerate(("g", "u", "d")):
            dh = pr.dram(f"d_w1{nm}_h", [RSH], F16)
            dl = pr.dram(f"d_w1{nm}_l", [RSH], BF16)
            w_sh.append((dh, dl))
            w_gl.append((pr.dram(f"g_w1{nm}_h", [8, RSH], F16),
                         pr.dram(f"g_w1{nm}_l", [8, RSH], BF16)))
        emit_recon(pr, taps, bks, w_sh)
        barrier(pr)
        recycle(pr, keep=("ag1", "ag2", "ag2b"))
        with nc.Block() as blkf:
            @blkf.gpsimd
            def _(g):
                for mi in range(3):
                    for j in range(2):
                        pr.inc(g.collective_compute(
                            "AllGather", Alu.bypass, RG,
                            ins=[w_sh[mi][j][:]], outs=[w_gl[mi][j][:, :]]), "ag4")

        dq = (pr.dram("d_q_h", [256, 8192], F16), pr.dram("d_q_l", [256, 8192], BF16))
        dk = (pr.dram("d_k_h", [256, 8192], F16), pr.dram("d_k_l", [256, 8192], BF16))
        dvn = (pr.dram("d_vn_h", [8192, 256], F16), pr.dram("d_vn_l", [8192, 256], BF16))
        for nm, pair in (("q", dq), ("k", dk), ("vn", dvn)):
            pr.maybe_dump(nm + "_h", pair[0])
            pr.maybe_dump(nm + "_l", pair[1])
        emit_qkv(pr, g_xnt_h, g_xnt_l, wspl, i_b, dq, dk, dvn, s_ident)
        barrier(pr)
        recycle(pr, keep=("ag2", "ag2b", "ag4"))

        i_masks = pr.inp("masks", [P, 4, 512], F32)
        d_at = pr.dram("d_at", [256, B, S], F32)
        pr.maybe_dump("at", d_at)
        emit_attn(pr, dq, dk, dvn, d_at, s_ident, i_masks)
        barrier(pr)
        recycle(pr, keep=("ag2", "ag2b", "ag4"))
        g_at = pr.dram("g_at", [8, 256, B, S], F32)
        with nc.Block() as blk3:
            @blk3.gpsimd
            def _(g):
                pr.inc(g.collective_compute("AllGather", Alu.bypass, RG,
                                            ins=[d_at[:, :, :]], outs=[g_at[:, :, :, :]]),
                       "ag3")

        d_qx = pr.dram("d_qx", [TOK, D], BF16)
        d_sx = pr.dram("d_sx", [TOK], F32)
        pr.maybe_dump("qx", d_qx)
        pr.maybe_dump("sx", d_sx)
        pr.maybe_dump("qwg", g_qwg)
        pr.maybe_dump("wsrow", d_wsrow)
        emit_quant(pr, "c2", d_xn, d_qx, d_sx, TOK, D)
        barrier(pr)
        recycle(pr, keep=("ag2", "ag2b", "ag3", "ag4"))
        d_t = pr.dram("d_t", [TOK, I0], F32)
        pr.maybe_dump("t", d_t)
        qwg_flat = g_qwg.rearrange("r a d -> (r a) d")
        qwu_flat = g_qwu.rearrange("r a d -> (r a) d")
        qwd_flat = g_qwd.rearrange("r a d -> (r a) d")
        emit_ffn0_gateup(pr, d_qx, d_sx, qwg_flat, qwu_flat, d_wsrow, d_t)
        barrier(pr)
        recycle(pr, keep=("ag2b", "ag3", "ag4"))
        d_qt = pr.dram("d_qt", [TOK, I0], BF16)
        d_st = pr.dram("d_st", [TOK], F32)
        emit_quant(pr, "c4", d_t, d_qt, d_st, TOK, I0)
        barrier(pr)
        recycle(pr, keep=("ag2b", "ag3", "ag4"))
        d_y0 = pr.dram("d_y0", [TOK, D], F32)
        pr.maybe_dump("y0", d_y0)
        emit_ffn0_down(pr, d_qt, d_st, qwd_flat, d_wsrow, d_y0)
        barrier(pr)
        recycle(pr, keep=("ag3", "ag4"))

        i_bo = pr.inp("bo", [1, D], F32)
        i_ones = pr.inp("ones_row", [1, P], F32)
        d_o = pr.dram("d_o", [TOK, D], F32)
        pr.maybe_dump("o", d_o)
        emit_oproj(pr, g_at, wspl["o"], i_bo, i_ones, d_o)
        barrier(pr)
        recycle(pr, keep=("ag4",))

        d_x1 = pr.dram("d_x1", [TOK, D], F32)
        pr.maybe_dump("x1", d_x1)
        emit_resid3(pr, "rd1", d_x, d_o, d_y0, d_x1)
        barrier(pr)
        recycle(pr, keep=("ag4",))

        d_xn1 = pr.dram("d_xn1", [TOK, D], F32)
        d_xn1t_h = pr.dram("d_xn1t_h", [D, TOK], F16)
        d_xn1t_l = pr.dram("d_xn1t_l", [D, TOK], BF16)
        pr.maybe_dump("xn1", d_xn1)
        emit_ln_xnt(pr, d_x1, d_xn1, d_xn1t_h, d_xn1t_l, s_ident, tag="e1")
        barrier(pr)
        recycle(pr, keep=("ag4",))

        e2_ins = {n: pr.inp(n, [P, DC, 8], F32) for n in ("arl", "aim", "cre", "cim")}
        e2_ins["ldt"] = pr.inp("ldt", [P, DC], F32)
        e2_ins["ident"] = i_ident
        d_k = pr.dram("d_k", [4, D], F32)
        pr.maybe_dump("k", d_k)
        emit_s4d_k(pr, e2_ins, d_k)
        barrier(pr)
        recycle(pr, keep=("ag4",))

        i_dpar = pr.inp("dpar", [1, D], F32)
        d_mix = pr.dram("d_mix", [TOK, D], F32)
        pr.maybe_dump("mix", d_mix)
        emit_mix(pr, d_xn1, d_k, i_dpar, d_mix)
        barrier(pr)
        recycle(pr, keep=("ag4",))

        w1g = tuple(w_gl[0][j].rearrange("r a -> (r a)").rearrange("(i h) -> i h", h=D)
                    for j in range(2))
        w1u = tuple(w_gl[1][j].rearrange("r a -> (r a)").rearrange("(i h) -> i h", h=D)
                    for j in range(2))
        w1d = tuple(w_gl[2][j].rearrange("r a -> (r a)").rearrange("(h i) -> h i", i=I1)
                    for j in range(2))
        d_t1 = (pr.dram("d_t1_h", [TOK, I1], F16), pr.dram("d_t1_l", [TOK, I1], BF16))
        emit_ffn1_gu(pr, (d_xn1t_h, d_xn1t_l), w1g, w1u, d_t1)
        barrier(pr)
        recycle(pr)
        d_y1 = pr.dram("d_y1", [TOK, D], F32)
        pr.maybe_dump("y1", d_y1)
        emit_ffn1_down(pr, d_t1, w1d, d_y1)
        barrier(pr)
        recycle(pr)

        o_out = pr.out("out", [B, 256, D], F32)
        emit_resid3(pr, "rd2", d_x1, d_mix, d_y1, o_out)
        barrier(pr)

        pr.emit_dumps()
    return pr


def _declared_inputs(pr):
    import concourse.mybir as mb
    names = set()
    for alloc in pr.nc.m.functions[0].allocations:
        if isinstance(alloc, mb.MemoryLocationSet) and alloc.kind == "ExternalInput":
            names.add(alloc.memorylocations[0].name)
    return names


def run(pr, inputs, **kw):
    in_maps_full = host_in_maps(inputs)
    names = _declared_inputs(pr)
    in_maps = [{k: v for k, v in m.items() if k in names} for m in in_maps_full]
    return run_bass_kernel_spmd(pr.nc, in_maps, core_ids=list(range(8)), **kw)


def kernel(**inputs):
    pr = build()
    res = run(pr, inputs)
    outs = [r["out"] for r in res.results]
    return np.concatenate(outs, axis=1)
